# version 04
# baseline (speedup 1.0000x reference)
import contextlib
import os
import numpy as np
try:
    import jax
    os.makedirs("/tmp/jaxcache", exist_ok=True)
    jax.config.update("jax_compilation_cache_dir", "/tmp/jaxcache")
    jax.config.update("jax_persistent_cache_min_compile_time_secs", 0.0)
    jax.config.update("jax_persistent_cache_min_entry_size_bytes", 0)
except Exception:
    pass
import ml_dtypes
import concourse.bass as bass
import concourse.mybir as mybir
from concourse.bass_utils import run_bass_kernel_spmd

F32 = mybir.dt.float32
BF16 = mybir.dt.bfloat16
AF = mybir.ActivationFunctionType
ALU = mybir.AluOpType
AX = mybir.AxisListType

D = 1024
T = 2048
NK = 8
NT = 4
TT = 512
NHEAD = 16
HD = 64
BF = ml_dtypes.bfloat16
LN_EPS = 1e-5


class Prog:
    def __init__(self, name):
        self.nc = bass.Bass("TRN2", target_bir_lowering=False)
        self.stack = contextlib.ExitStack()
        self.blk = None
        self.flow = None
        self.dflow = None
        self.sflow = None
        self.val = 0
        self.dval = 0
        self.sval = 0

    def start(self):
        self.flow = self.stack.enter_context(self.nc.semaphore("flow"))
        self.dflow = self.stack.enter_context(self.nc.semaphore("dflow"))
        self.sflow = self.stack.enter_context(self.nc.semaphore("sflow"))
        self._chain = self.stack.enter_context(self.nc.semaphore("chain"))
        self._chain_n = 0
        self.blk = self.stack.enter_context(self.nc.Block())
        return self

    def sbuf(self, name, shape, dtype):
        return self.stack.enter_context(self.nc.sbuf_tensor(name, shape, dtype))

    def psum(self, name, shape, dtype=F32):
        return self.stack.enter_context(self.nc.psum_tensor(name, shape, dtype))

    def sem(self, name):
        return self.stack.enter_context(self.nc.semaphore(name))

    def sec(self, eng, fn):
        v, dv, sv = self.val, self.dval, self.sval
        spaced = eng in ('vector', 'scalar', 'gpsimd')

        def body(e):
            if v > 0:
                e.wait_ge(self.flow, v)
            if dv > 0:
                e.wait_ge(self.dflow, dv)
            if sv > 0:
                e.wait_ge(self.sflow, sv)
            ee = _Spaced(e, self) if spaced else e
            last = fn(ee)
            if isinstance(last, list):
                last = last[-1]
            last.then_inc(self.flow, 1)

        getattr(self.blk, eng)(body)
        self.val = v + 1

    def dma_sec(self, eng, fn):
        v = self.val

        def body(e):
            if v > 0:
                e.wait_ge(self.flow, v)
            for ins in fn(e):
                ins.then_inc(self.flow, 16)

        getattr(self.blk, eng)(body)
        self.n_last_dma = None
        raise RuntimeError("use dma_sec_n")

    def dma_sec_n(self, eng, n, fn):
        v, dv, sv = self.val, self.dval, self.sval
        soft = (eng == 'gpsimd')
        sem = self.sflow if soft else self.dflow

        def body(e):
            if v > 0:
                e.wait_ge(self.flow, v)
            if dv > 0:
                e.wait_ge(self.dflow, dv)
            if sv > 0:
                e.wait_ge(self.sflow, sv)
            ins_list = fn(e)
            assert len(ins_list) == n, (len(ins_list), n)
            for ins in ins_list:
                ins.then_inc(sem, 16)

        getattr(self.blk, eng)(body)
        if soft:
            self.sval = sv + 16 * n
        else:
            self.dval = dv + 16 * n

    def fin(self):
        v, dv, sv = self.val, self.dval, self.sval

        def body(e):
            if v > 0:
                e.wait_ge(self.flow, v)
            if dv > 0:
                e.wait_ge(self.dflow, dv)
            if sv > 0:
                e.wait_ge(self.sflow, sv)

        self.blk.sync(body)

    def close(self):
        self.stack.close()
        return self.nc


_SPACED_OPS = {
    'tensor_tensor', 'tensor_add', 'tensor_sub', 'tensor_mul', 'tensor_max',
    'tensor_scalar', 'tensor_single_scalar', 'tensor_scalar_add',
    'tensor_scalar_sub', 'tensor_scalar_mul', 'tensor_scalar_max',
    'tensor_scalar_min', 'tensor_relu', 'tensor_copy', 'tensor_reduce',
    'reduce_max', 'reduce_sum', 'select', 'copy_predicated', 'reciprocal',
    'memset', 'memzero', 'scalar_tensor_tensor', 'tensor_tensor_scan',
    'activation', 'square', 'sqrt', 'copy', 'mul', 'add', 'sign', 'transpose',
}


class _Spaced:
    def __init__(self, e, prog):
        self._e = e
        self._p = prog
        self._pend = None

    def __getattr__(self, name):
        attr = getattr(self._e, name)
        if name not in _SPACED_OPS:
            return attr

        def wrapped(*a, **k):
            if self._pend is not None:
                self._pend.then_inc(self._p._chain, 1)
                self._p._chain_n += 1
                self._e.wait_ge(self._p._chain, self._p._chain_n)
            ins = attr(*a, **k)
            self._pend = ins
            return ins
        return wrapped


def ceil_div(a, b):
    return (a + b - 1) // b


def emit_ln(p, xs, lnout, sq, work, ps_sx, ps_sxx, ps_bc, ones_col, ones_row,
            eps=LN_EPS):
    rb, mb = p._ln_rb, p._ln_mb
    W = lambda s, q=None: work[0:1, s * T + (0 if q is None else q * TT):
                               s * T + (T if q is None else (q + 1) * TT)]

    for q in range(NT):
        qsl = slice(q * TT, (q + 1) * TT)

        def t_sx(te, q=q, qsl=qsl):
            for k in range(NK):
                ins = te.matmul(ps_sx[0:1, :], ones_col[:, 0:1], xs[:, k, qsl],
                                start=(k == 0), stop=(k == NK - 1))
            return ins
        p.sec('tensor', t_sx)
        for kp in range(0, NK, 2):
            p.sec('scalar', lambda s, kp=kp, qsl=qsl: [
                s.square(sq[:, (kp + j) % 2, :], xs[:, kp + j, qsl])
                for j in range(2)])
            p.sec('tensor', lambda te, kp=kp: [
                te.matmul(ps_sxx[0:1, :], ones_col[:, 0:1], sq[:, (kp + j) % 2, :],
                          start=(kp + j == 0), stop=(kp + j == NK - 1))
                for j in range(2)])

        def v_q(v, q=q):
            v.tensor_scalar_mul(W(0, q), ps_sx[0:1, :], 1.0 / D)
            return v.tensor_scalar_mul(W(1, q), ps_sxx[0:1, :], 1.0 / D)
        p.sec('vector', v_q)

    def v_stats(v):
        v.tensor_tensor(W(2), W(0), W(0), ALU.mult)
        v.tensor_tensor(W(1), W(1), W(2), ALU.subtract)
        return v.tensor_scalar_add(W(1), W(1), eps)
    p.sec('vector', v_stats)

    p.sec('scalar', lambda s: s.activation(W(2), W(1), AF.Sqrt))

    def v_fin(v):
        v.reciprocal(W(1), W(2))
        return v.tensor_tensor(W(2), W(0), W(1), ALU.mult)
    p.sec('vector', v_fin)

    for q in range(NT):
        def t_bq(te, q=q):
            te.matmul(ps_bc[:, 0:TT], ones_row[0:1, :], W(1, q),
                      start=True, stop=True)
            return te.matmul(ps_bc[:, TT:2 * TT], ones_row[0:1, :], W(2, q),
                             start=True, stop=True)
        p.sec('tensor', t_bq)

        def v_bq(v, q=q):
            v.tensor_copy(rb[:, q * TT:(q + 1) * TT], ps_bc[:, 0:TT])
            return v.tensor_copy(mb[:, q * TT:(q + 1) * TT], ps_bc[:, TT:2 * TT])
        p.sec('vector', v_bq)

    for k in range(NK):
        def v_apply(v, k=k):
            out = []
            for q in range(NT):
                qsl = slice(q * TT, (q + 1) * TT)
                v.tensor_tensor(sq[:, 0, :], xs[:, k, qsl], rb[:, qsl], ALU.mult)
                out.append(v.tensor_tensor(lnout[:, k, qsl], sq[:, 0, :],
                                           mb[:, qsl], ALU.subtract))
            return out
        p.sec('vector', v_apply)


def alloc_ln(p):
    p._ln_rb = p.sbuf("ln_rb", [128, T], F32)
    p._ln_mb = p.sbuf("ln_mb", [128, T], F32)


def emit_softmax_gates(p, logits_fm, gates_fm, ptm, ps8, identf, negt, mode):
    E = 8
    for i in range(16):
        p.sec('tensor', lambda te, i=i: te.transpose(
            ps8[:, 0:E], logits_fm[0:E, i * 128:(i + 1) * 128], identf[0:E, 0:E]))
        p.sec('vector', lambda v, i=i: v.tensor_copy(ptm[:, i, 0:E], ps8[:, 0:E]))

    def v_sm1(v):
        out = []
        for i in range(16):
            v.reduce_max(ptm[:, i, E:E + 1], ptm[:, i, 0:E], AX.X)
            out.append(v.tensor_scalar_mul(ptm[:, i, E + 1:E + 2],
                                           ptm[:, i, E:E + 1], -1.0))
        return out
    p.sec('vector', v_sm1)

    p.sec('scalar', lambda s: [s.activation(ptm[:, i, 0:E], ptm[:, i, 0:E], AF.Exp,
                                            bias=ptm[:, i, E + 1:E + 2])
                               for i in range(16)])

    def v_sm2(v):
        out = []
        for i in range(16):
            v.reduce_sum(ptm[:, i, E:E + 1], ptm[:, i, 0:E], AX.X)
            v.reciprocal(ptm[:, i, E + 1:E + 2], ptm[:, i, E:E + 1])
            out.append(v.tensor_scalar_mul(ptm[:, i, 0:E], ptm[:, i, 0:E],
                                           ptm[:, i, E + 1:E + 2]))
        return out
    p.sec('vector', v_sm2)

    if mode == 'top2':
        def v_top2(v):
            out = []
            for i in range(16):
                tile = ptm[:, i, 0:E]
                m1 = ptm[:, i, E:E + 1]
                m2 = ptm[:, i, E + 1:E + 2]
                msk = ptm[:, i, E + 2:E + 2 + E]
                tmp = ptm[:, i, E + 2 + E:E + 2 + 2 * E]
                v.reduce_max(m1, tile, AX.X)
                v.tensor_scalar(msk, tile, m1, None, ALU.is_ge)
                v.tensor_scalar_add(tmp, tile, 1e30)
                v.tensor_tensor(tmp, msk, tmp, ALU.mult)
                v.tensor_tensor(tmp, tile, tmp, ALU.subtract)
                v.reduce_max(m2, tmp, AX.X)
                v.tensor_scalar(msk, tile, m2, None, ALU.is_ge)
                out.append(v.tensor_tensor(tile, tile, msk, ALU.mult))
            return out
        p.sec('vector', v_top2)

    for i in range(16):
        p.sec('tensor', lambda te, i=i: te.transpose(
            ps8[0:E, 0:128], ptm[:, i, 0:E], identf[:, :]))
        p.sec('vector', lambda v, i=i: v.tensor_copy(
            gates_fm[0:E, i * 128:(i + 1) * 128], ps8[0:E, 0:128]))


def emit_logits(p, lnbf, wT_sb, ncols, ps, out_fm, bias_sb=None):
    for q in range(NT):
        def t_q(te, q=q):
            for k in range(NK):
                ins = te.matmul(ps[0:ncols, 0:TT], wT_sb[:, k, 0:ncols],
                                lnbf[:, k, q * TT:(q + 1) * TT],
                                start=(k == 0), stop=(k == NK - 1))
            return ins
        p.sec('tensor', t_q)
        if bias_sb is not None:
            p.sec('scalar', lambda s, q=q: s.activation(
                out_fm[0:ncols, q * TT:(q + 1) * TT], ps[0:ncols, 0:TT],
                AF.Identity, bias=bias_sb[0:ncols, 0:1]))
        else:
            p.sec('vector', lambda v, q=q: v.tensor_copy(
                out_fm[0:ncols, q * TT:(q + 1) * TT], ps[0:ncols, 0:TT]))


def build_stage(n_partials, heads):
    p = Prog("stage").start()
    nc = p.nc
    xT = nc.dram_tensor("xT", [D, T], F32, kind="ExternalInput").ap()
    if n_partials:
        pp = nc.dram_tensor("pparts", [n_partials, D, T], F32,
                            kind="ExternalInput").ap()
    xs = p.sbuf("xs", [128, NK, T], F32)
    sq = p.sbuf("sq", [128, 3, TT], F32)
    lnbf = p.sbuf("lnbf", [128, NK, T], BF16)
    work = p.sbuf("work", [1, 3 * T], F32)
    ones_col = p.sbuf("ones_col", [128, 1], F32)
    ones_row = p.sbuf("ones_row", [1, 128], F32)
    alloc_ln(p)
    ps_sx = p.psum("ps_sx", [1, TT], F32)
    ps_sxx = p.psum("ps_sxx", [1, TT], F32)
    ps_bc = p.psum("ps_bc", [128, 2 * TT], F32)
    ps_a = p.psum("ps_a", [128, TT], F32)
    ps_b = ps_a

    outs = {}

    p.sec('vector', lambda v: [v.memset(ones_col[:, :], 1.0),
                               v.memset(ones_row[:, :], 1.0)])

    p.dma_sec_n('sync', NK, lambda e: [
        e.dma_start(out=xs[:, k, :], in_=xT[k * 128:(k + 1) * 128, :])
        for k in range(NK)])

    if n_partials:
        ptmp = p.sbuf("ptmp", [128, T], F32)
        for k in range(NK):
            for j in range(n_partials):
                p.dma_sec_n('sync', 1, lambda e, k=k, j=j: [
                    e.dma_start(out=ptmp[:, :], in_=pp[j, k * 128:(k + 1) * 128, :])])
                p.sec('vector', lambda v, k=k: v.scalar_tensor_tensor(
                    xs[:, k, :], ptmp[:, :], 0.5, xs[:, k, :], ALU.mult, ALU.add))

    if 'out_x' in heads:
        xo = nc.dram_tensor("x_out", [D, T], F32, kind="ExternalOutput").ap()
        p.dma_sec_n('sync', NK, lambda e: [
            e.dma_start(out=xo[k * 128:(k + 1) * 128, :], in_=xs[:, k, :])
            for k in range(NK)])

    emit_ln(p, xs, lnbf, sq, work, ps_sx, ps_sxx, ps_bc, ones_col, ones_row)

    if 'gates0' in heads or 'probs1' in heads:
        mode = 'top2' if 'gates0' in heads else 'probs'
        gw = nc.dram_tensor("gwT", [D, 8], F32, kind="ExternalInput").ap()
        idf = nc.dram_tensor("identf", [128, 128], F32, kind="ExternalInput").ap()
        gates_out = nc.dram_tensor("gates", [8, T], F32, kind="ExternalOutput").ap()
        lnout_hbm = nc.dram_tensor("lnbf_out", [D, T], BF16, kind="ExternalOutput").ap()
        gw_sb = p.sbuf("gw_sb", [128, NK, 8], BF16)
        idf_sb = p.sbuf("idf_sb", [128, 128], F32)
        gfm = p.sbuf("gfm", [8, T], F32)
        ptm = p.sbuf("ptm", [128, 16, 26], F32)
        negt = p.sbuf("negt", [128, 8], F32)
        gb_sb = p.sbuf("gb_sb", [8, 1], F32)
        if mode == 'top2':
            gb = nc.dram_tensor("gb", [8], F32, kind="ExternalInput").ap()
            p.dma_sec_n('sync', 2, lambda e: [
                e.dma_start(out=idf_sb[:, :], in_=idf[:, :]),
                e.dma_start(out=gb_sb[:, 0], in_=gb[:])])
        else:
            p.dma_sec_n('sync', 1, lambda e: [
                e.dma_start(out=idf_sb[:, :], in_=idf[:, :])])
        p.dma_sec_n('gpsimd', 1, lambda e: [
            e.dma_start(out=gw_sb[:, :, :],
                        in_=gw.rearrange("(k p) e -> p k e", p=128))])
        p.sec('vector', lambda v: v.memset(negt[:, :], -1e30))
        emit_logits(p, lnbf, gw_sb, 8, ps_a, gfm,
                    bias_sb=gb_sb if mode == 'top2' else None)
        emit_softmax_gates(p, gfm, gfm, ptm, ps_bc, idf_sb, negt, mode)
        p.dma_sec_n('sync', 1 + NK, lambda e: [
            e.dma_start(out=gates_out[:, :], in_=gfm[:, :])] + [
            e.dma_start(out=lnout_hbm[k * 128:(k + 1) * 128, :], in_=lnbf[:, k, :])
            for k in range(NK)])

    if 'projL0' in heads:
        emit_projL0(p, nc, lnbf, ps_a, ps_bc, ones_row)

    if 'qkv1' in heads:
        emit_qkv1(p, nc, lnbf, xs, sq, work, ps_a, ps_b)

    p.fin()
    return p.close()


def emit_projL0(p, nc, lnbf, ps_a, ps_bc, ones_row):
    wq = nc.dram_tensor("WqT", [D, D], F32, kind="ExternalInput").ap()
    wk = nc.dram_tensor("WkT", [D, D], F32, kind="ExternalInput").ap()
    wv = nc.dram_tensor("WvT", [D, D], F32, kind="ExternalInput").ap()
    wb = nc.dram_tensor("WbetaT", [D, NHEAD], F32, kind="ExternalInput").ap()
    bd2 = nc.dram_tensor("blockdiag2", [128, 2], F32, kind="ExternalInput").ap()
    bc64 = nc.dram_tensor("bcast64", [2, 128], F32, kind="ExternalInput").ap()
    idb = nc.dram_tensor("identf", [128, 128], F32, kind="ExternalInput").ap()
    q_lm = nc.dram_tensor("q_lm", [NHEAD, T, HD], F32, kind="ExternalOutput").ap()
    k_lm = nc.dram_tensor("k_lm", [NHEAD, T, HD], F32, kind="ExternalOutput").ap()
    v_lm = nc.dram_tensor("v_lm", [NHEAD, T, HD], F32, kind="ExternalOutput").ap()
    beta_o = nc.dram_tensor("beta", [NHEAD, T], F32, kind="ExternalOutput").ap()

    w_sb = p.sbuf("w_sb", [128, NK, D], BF16)
    pt = p.sbuf("pt", [128, TT], F32)
    sqt = p.sbuf("sqt", [128, TT], F32)
    hsum = p.sbuf("hsum", [2, TT], F32)
    obf = p.sbuf("obf", [128, NK, TT], F32)
    tmt = p.sbuf("tmt", [128, NHEAD * HD], F32)
    bd2_sb = p.sbuf("bd2_sb", [128, 2], F32)
    bc64_sb = p.sbuf("bc64_sb", [2, 128], F32)
    idb_sb = p.sbuf("idb_sb", [128, 128], F32)
    bsb = p.sbuf("bsb", [NHEAD, TT], F32)
    wbsb = p.sbuf("wbsb", [128, NK, NHEAD], BF16)
    ps_t = p.psum("ps_t", [128, HD], F32)

    p.dma_sec_n('sync', 3, lambda e: [
        e.dma_start(out=bd2_sb[:, :], in_=bd2[:, :]),
        e.dma_start(out=bc64_sb[:, :], in_=bc64[:, :]),
        e.dma_start(out=idb_sb[:, :], in_=idb[:, :])])

    def proj_one(wap, out_hbm, do_l2):
        p.dma_sec_n('gpsimd', 1, lambda e: [
            e.dma_start(out=w_sb[:, :, :],
                        in_=wap.rearrange("(k p) o -> p k o", p=128))])
        for tt_i in range(NT):
            tsl = slice(tt_i * TT, (tt_i + 1) * TT)
            for oc in range(NK):
                def t_mm(te, oc=oc, tsl=tsl):
                    for k in range(NK):
                        ins = te.matmul(ps_a[:, :], w_sb[:, k, oc * 128:(oc + 1) * 128],
                                        lnbf[:, k, tsl], start=(k == 0),
                                        stop=(k == NK - 1))
                    return ins
                p.sec('tensor', t_mm)
                if do_l2:
                    p.sec('scalar', lambda s: s.activation(pt[:, :], ps_a[:, :], AF.Silu))
                    p.sec('scalar', lambda s: s.square(sqt[:, :], pt[:, :]))
                    p.sec('tensor', lambda te: te.matmul(
                        ps_bc[0:2, 0:TT], bd2_sb[:, :], sqt[:, :],
                        start=True, stop=True))
                    p.sec('vector', lambda v: v.tensor_copy(hsum[:, :], ps_bc[0:2, 0:TT]))
                    p.sec('scalar', lambda s: s.activation(hsum[:, :], hsum[:, :], AF.Sqrt))
                    p.sec('vector', lambda v: [
                        v.tensor_scalar_add(hsum[:, :], hsum[:, :], 1e-6),
                        v.reciprocal(hsum[:, :], hsum[:, :])])
                    p.sec('tensor', lambda te: te.matmul(
                        ps_bc[:, 0:TT], bc64_sb[:, :], hsum[:, :],
                        start=True, stop=True))
                    p.sec('vector', lambda v, oc=oc: v.tensor_tensor(
                        obf[:, oc, :], pt[:, :], ps_bc[:, 0:TT], ALU.mult))
                else:
                    p.sec('scalar', lambda s, oc=oc: s.activation(
                        obf[:, oc, :], ps_a[:, :], AF.Silu))
            for c4 in range(4):
                c = tt_i * 4 + c4
                for lane in range(NHEAD):
                    oc, half = lane // 2, (lane % 2) * 64
                    p.sec('tensor', lambda te, oc=oc, half=half, c4=c4: te.transpose(
                        ps_t[:, :], obf[half:half + 64, oc, c4 * 128:(c4 + 1) * 128],
                        idb_sb[half:half + 64, half:half + 64]))
                    p.sec('vector', lambda v, lane=lane: v.tensor_copy(
                        tmt[:, lane * HD:(lane + 1) * HD], ps_t[:, :]))
                p.dma_sec_n('sync', 1, lambda e, c=c: [
                    e.dma_start(out=out_hbm[:, c * 128:(c + 1) * 128, :]
                                .rearrange("l p d -> p l d"),
                                in_=tmt.rearrange("p (l d) -> p l d", l=NHEAD))])

    proj_one(wq, q_lm, True)
    proj_one(wk, k_lm, True)
    proj_one(wv, v_lm, False)

    p.dma_sec_n('gpsimd', 1, lambda e: [
        e.dma_start(out=wbsb[:, :, :], in_=wb.rearrange("(k p) e -> p k e", p=128))])
    for q in range(NT):
        def t_b(te, q=q):
            for k in range(NK):
                ins = te.matmul(ps_a[0:NHEAD, :], wbsb[:, k, :],
                                lnbf[:, k, q * TT:(q + 1) * TT],
                                start=(k == 0), stop=(k == NK - 1))
            return ins
        p.sec('tensor', t_b)
        p.sec('scalar', lambda s: s.activation(bsb[:, :], ps_a[0:NHEAD, :], AF.Sigmoid))
        p.dma_sec_n('sync', 1, lambda e, q=q: [
            e.dma_start(out=beta_o[:, q * TT:(q + 1) * TT], in_=bsb[:, :])])


def emit_qkv1(p, nc, lnbf, xs, sq, work, ps_a, ps_b):
    wqkv = nc.dram_tensor("wqkvP", [D, 3 * D], F32, kind="ExternalInput").ap()
    xln_o = nc.dram_tensor("xln_out", [D, T], F32, kind="ExternalOutput").ap()
    outs = [nc.dram_tensor(n, [D, T], BF16, kind="ExternalOutput").ap()
            for n in ("q4", "k4", "v4")]
    w_sb = p.sbuf("wqkv_sb", [128, NK, D], BF16)
    ofm = p.sbuf("ofm", [128, NK, TT], BF16)
    xlt = p.sbuf("xlt", [128, T], F32)

    for k in range(NK):
        def v_xln(v, k=k):
            v.tensor_tensor(xlt[:, :], xs[:, k, :], p._ln_rb[:, :], ALU.mult)
            return v.tensor_tensor(xlt[:, :], xlt[:, :], p._ln_mb[:, :],
                                   ALU.subtract)
        p.sec('vector', v_xln)
        p.dma_sec_n('sync', 1, lambda e, k=k: [
            e.dma_start(out=xln_o[k * 128:(k + 1) * 128, :], in_=xlt[:, :])])

    for oi, oap in enumerate(outs):
        p.dma_sec_n('gpsimd', 1, lambda e, oi=oi: [
            e.dma_start(out=w_sb[:, :, :],
                        in_=wqkv[:, oi * D:(oi + 1) * D]
                        .rearrange("(k p) o -> p k o", p=128))])
        for tt_i in range(NT):
            for oc in range(NK):
                psx = ps_a if oc % 2 == 0 else ps_b

                def t_mm(te, oc=oc, tt_i=tt_i, psx=psx):
                    for k in range(NK):
                        ins = te.matmul(psx[:, :], w_sb[:, k, oc * 128:(oc + 1) * 128],
                                        lnbf[:, k, tt_i * TT:(tt_i + 1) * TT],
                                        start=(k == 0), stop=(k == NK - 1))
                    return ins
                p.sec('tensor', t_mm)
                p.sec('vector', lambda v, oc=oc, psx=psx: v.tensor_copy(
                    ofm[:, oc, :], psx[:, :]))
            p.dma_sec_n('sync', 1, lambda e, tt_i=tt_i: [
                e.dma_start(out=oap[:, tt_i * TT:(tt_i + 1) * TT]
                            .rearrange("(k p) c -> p k c", p=128),
                            in_=ofm[:, :, :])])


def build_experts(kind, ne):
    H = 4096 if kind == 'L0' else 2048
    NH = H // 128
    act = AF.Relu if kind == 'L0' else AF.Gelu
    p = Prog("exp").start()
    nc = p.nc
    lnin = nc.dram_tensor("lnbf_in", [D, T], BF16, kind="ExternalInput").ap()
    gts = nc.dram_tensor("gates", [8, T], F32, kind="ExternalInput").ap()
    sel = nc.dram_tensor("sel2", [8, ne * 128], F32, kind="ExternalInput").ap()
    w1 = nc.dram_tensor("w1", [ne, D, H], F32, kind="ExternalInput").ap()
    w2 = nc.dram_tensor("w2", [ne, H, D], F32, kind="ExternalInput").ap()
    if kind == 'L0':
        b1 = nc.dram_tensor("b1", [ne, 128, NH], F32, kind="ExternalInput").ap()
        b2 = nc.dram_tensor("b2", [ne, 128, 8], F32, kind="ExternalInput").ap()
    parts = nc.dram_tensor("partials", [ne, D, T], F32, kind="ExternalOutput").ap()

    w1s = p.sbuf("w1s", [128, NK, H], BF16)
    w2s = p.sbuf("w2s", [128, NH, D], BF16)
    lnt = p.sbuf("lnt", [128, NK, TT], BF16)
    hs = p.sbuf("hs", [128, NH, TT], BF16)
    ytmp = p.sbuf("ytmp", [128, TT], F32)
    ybuf = p.sbuf("ybuf", [128, NK, TT], F32)
    gateb = p.sbuf("gateb", [128, TT], F32)
    gts_sb = p.sbuf("gts_sb", [8, T], F32)
    sel_sb = p.sbuf("sel_sb", [8, ne * 128], F32)
    if kind == 'L0':
        b1s = p.sbuf("b1s", [128, NH], F32)
        b2s = p.sbuf("b2s", [128, 8], F32)
    psA = p.psum("psA", [128, TT], F32)
    psB = p.psum("psB", [128, TT], F32)
    psG = p.psum("psG", [128, TT], F32)

    p.dma_sec_n('sync', 2, lambda e: [
        e.dma_start(out=gts_sb[:, :], in_=gts[:, :]),
        e.dma_start(out=sel_sb[:, :], in_=sel[:, :])])

    for i in range(ne):
        nw = 1 + 4 + (2 if kind == 'L0' else 0)

        def gw(e, i=i):
            out = [e.dma_start(out=w1s[:, :, :],
                               in_=w1[i, :, :].rearrange("(k p) h -> p k h", p=128))]
            for j in range(4):
                lo = NH // 4 * j
                hi = NH // 4 * (j + 1)
                out.append(e.dma_start(
                    out=w2s[:, lo:hi, :],
                    in_=w2[i, lo * 128:hi * 128, :].rearrange("(k p) o -> p k o", p=128)))
            if kind == 'L0':
                out.append(e.dma_start(out=b1s[:, :], in_=b1[i, :, :]))
                out.append(e.dma_start(out=b2s[:, :], in_=b2[i, :, :]))
            return out
        p.dma_sec_n('gpsimd', nw, gw)

        for tt_i in range(NT):
            tsl = slice(tt_i * TT, (tt_i + 1) * TT)
            p.dma_sec_n('sync', 1, lambda e, tsl=tsl: [
                e.dma_start(out=lnt[:, :, :],
                            in_=lnin[:, tsl].rearrange("(k p) c -> p k c", p=128))])
            p.sec('tensor', lambda te, i=i, tsl=tsl: te.matmul(
                psG[:, :], sel_sb[:, i * 128:(i + 1) * 128], gts_sb[:, tsl],
                start=True, stop=True))
            p.sec('vector', lambda v: v.tensor_copy(gateb[:, :], psG[:, :]))
            for hc in range(NH):
                psx = psA if hc % 2 == 0 else psB

                def t_h(te, hc=hc, psx=psx):
                    for k in range(NK):
                        ins = te.matmul(psx[:, :], w1s[:, k, hc * 128:(hc + 1) * 128],
                                        lnt[:, k, :], start=(k == 0), stop=(k == NK - 1))
                    return ins
                p.sec('tensor', t_h)
                if kind == 'L0':
                    p.sec('scalar', lambda s, hc=hc, psx=psx: s.activation(
                        hs[:, hc, :], psx[:, :], act, bias=b1s[:, hc:hc + 1]))
                else:
                    p.sec('scalar', lambda s, hc=hc, psx=psx: s.activation(
                        hs[:, hc, :], psx[:, :], act))
            for oc in range(NK):
                psx = psA if oc % 2 == 0 else psB

                def t_o(te, oc=oc, psx=psx):
                    for k in range(NH):
                        ins = te.matmul(psx[:, :], w2s[:, k, oc * 128:(oc + 1) * 128],
                                        hs[:, k, :], start=(k == 0), stop=(k == NH - 1))
                    return ins
                p.sec('tensor', t_o)
                if kind == 'L0':
                    p.sec('scalar', lambda s, oc=oc, psx=psx: s.activation(
                        ytmp[:, :], psx[:, :], AF.Identity, bias=b2s[:, oc:oc + 1]))
                else:
                    p.sec('scalar', lambda s, oc=oc, psx=psx: s.copy(
                        ytmp[:, :], psx[:, :]))
                p.sec('vector', lambda v, oc=oc: v.tensor_tensor(
                    ybuf[:, oc, :], ytmp[:, :], gateb[:, :], ALU.mult))
            p.dma_sec_n('sync', 1, lambda e, i=i, tsl=tsl: [
                e.dma_start(out=parts[i, :, tsl].rearrange("(k p) c -> p k c", p=128),
                            in_=ybuf[:, :, :])])
    p.fin()
    return p.close()


def build_scan():
    p = Prog("scan").start()
    nc = p.nc
    qp = nc.dram_tensor("qp", [NHEAD, 512, HD], F32, kind="ExternalInput").ap()
    kp = nc.dram_tensor("kp", [NHEAD, 512, HD], F32, kind="ExternalInput").ap()
    vp = nc.dram_tensor("vp", [NHEAD, 512, HD], F32, kind="ExternalInput").ap()
    bp = nc.dram_tensor("betap", [NHEAD, 512], F32, kind="ExternalInput").ap()
    mi = nc.dram_tensor("M_in", [NHEAD, HD * HD], F32, kind="ExternalInput").ap()
    mo = nc.dram_tensor("M_out", [NHEAD, HD * HD], F32, kind="ExternalOutput").ap()
    op = nc.dram_tensor("o_part", [512, D], F32, kind="ExternalOutput").ap()

    M = p.sbuf("M", [NHEAD, HD, HD], F32)
    qc = p.sbuf("qc", [NHEAD, 128, HD], F32)
    kc = p.sbuf("kc", [NHEAD, 128, HD], F32)
    vc_t = p.sbuf("vc", [NHEAD, 128, HD], F32)
    bsb = p.sbuf("bsb", [NHEAD, 512], F32)
    osb = p.sbuf("osb", [NHEAD, 128, HD], F32)
    s1 = p.sbuf("s1", [NHEAD, HD, HD], F32)
    s2 = p.sbuf("s2", [NHEAD, HD, HD], F32)
    s3 = p.sbuf("s3", [NHEAD, HD, HD], F32)
    gdum = p.sbuf("gdum", [1, 2], F32)
    dl = p.sbuf("dl", [NHEAD, 2, HD], F32)

    p.sec('vector', lambda v: v.memset(gdum[0:1, :], 0.0))
    p.dma_sec_n('sync', 2, lambda e: [
        e.dma_start(out=M.rearrange("p a b -> p (a b)")[:, :], in_=mi[:, :]),
        e.dma_start(out=bsb[:, :], in_=bp[:, :])])

    Mba = M.rearrange("p a b -> p b a")
    vc = p.sem("vc")
    gs = p.sem("gs")

    for c in range(4):
        csl = slice(c * 128, (c + 1) * 128)
        p.dma_sec_n('sync', 3, lambda e, csl=csl: [
            e.dma_start(out=qc[:, :, :], in_=qp[:, csl, :]),
            e.dma_start(out=kc[:, :, :], in_=kp[:, csl, :]),
            e.dma_start(out=vc_t[:, :, :], in_=vp[:, csl, :])])

        v0 = p.val
        dv0, sv0 = p.dval, p.sval
        vbase = (5 * 128 - 1) * c
        gbase = 2 * 128 * c

        def vsec(v, c=c, v0=v0, dv0=dv0, vbase=vbase, gbase=gbase):
            if v0 > 0:
                v.wait_ge(p.flow, v0)
            if dv0 > 0:
                v.wait_ge(p.dflow, dv0)
            n = vbase
            for tl in range(128):
                t = c * 128 + tl
                k_t = kc[:, tl, :]
                v_t = vc_t[:, tl, :]
                if n > 0:
                    v.wait_ge(vc, n)
                v.wait_ge(gs, gbase + 2 * tl + 1)
                v.tensor_reduce(osb[:, tl, :], s1[:, :, :], AX.X,
                                ALU.add).then_inc(vc, 1); n += 1
                v.wait_ge(vc, n)
                v.wait_ge(gs, gbase + 2 * tl + 2)
                v.tensor_reduce(dl[:, 0, :], s2[:, :, :], AX.X,
                                ALU.add).then_inc(vc, 1); n += 1
                v.wait_ge(vc, n)
                v.tensor_tensor(dl[:, 1, :], v_t, dl[:, 0, :],
                                ALU.subtract).then_inc(vc, 1); n += 1
                v.wait_ge(vc, n)
                v.scalar_tensor_tensor(
                    s3[:, :, :],
                    k_t.unsqueeze(2).broadcast_to([NHEAD, HD, HD]),
                    bsb[:, t:t + 1],
                    dl[:, 1, :].unsqueeze(1).broadcast_to([NHEAD, HD, HD]),
                    ALU.mult, ALU.mult).then_inc(vc, 1); n += 1
                v.wait_ge(vc, n)
                ins = v.tensor_tensor(M[:, :, :], M[:, :, :], s3[:, :, :],
                                      ALU.add)
                if tl < 127:
                    ins.then_inc(vc, 1)
                n += 1
            return ins

        def gsec(g, c=c, v0=v0, dv0=dv0, vbase=vbase, gbase=gbase):
            if v0 > 0:
                g.wait_ge(p.flow, v0)
            if dv0 > 0:
                g.wait_ge(p.dflow, dv0)
            m = gbase
            for tl in range(128):
                q_t = qc[:, tl, :]
                k_t = kc[:, tl, :]
                if m > 0:
                    g.wait_ge(gs, m)
                g.wait_ge(vc, vbase + 5 * tl)
                g.tensor_tensor(s1[:, :, :], Mba,
                                q_t.unsqueeze(1).broadcast_to([NHEAD, HD, HD]),
                                ALU.mult).then_inc(gs, 1); m += 1
                g.wait_ge(gs, m)
                g.tensor_tensor(s2[:, :, :], M[:, :, :],
                                k_t.unsqueeze(1).broadcast_to([NHEAD, HD, HD]),
                                ALU.mult).then_inc(gs, 1)
                m += 1
            g.wait_ge(gs, m)
            return g.tensor_copy(gdum[0:1, 0:1], gdum[0:1, 1:2])

        def vbody(v, fn=vsec):
            fn(v).then_inc(p.flow, 1)

        def gbody(g, fn=gsec):
            fn(g).then_inc(p.flow, 1)

        p.blk.vector(vbody)
        p.blk.gpsimd(gbody)
        p.val = v0 + 2

        p.dma_sec_n('sync', 1, lambda e, csl=csl: [
            e.dma_start(out=op[csl, :].rearrange("t (l d) -> l t d", l=NHEAD),
                        in_=osb[:, :, :])])

    p.dma_sec_n('sync', 1, lambda e: [
        e.dma_start(out=mo[:, :], in_=M.rearrange("p a b -> p (a b)")[:, :])])
    p.fin()
    return p.close()


def build_rmswo():
    p = Prog("rmswo").start()
    nc = p.nc
    of = nc.dram_tensor("o_full", [T, D], F32, kind="ExternalInput").ap()
    x1 = nc.dram_tensor("xT", [D, T], F32, kind="ExternalInput").ap()
    wo = nc.dram_tensor("WoT", [D, D], F32, kind="ExternalInput").ap()
    rw = nc.dram_tensor("rmsw_rep", [128, D], F32, kind="ExternalInput").ap()
    idb = nc.dram_tensor("identb", [128, 128], BF16, kind="ExternalInput").ap()

    xs = p.sbuf("xs", [128, NK, T], F32)
    sq = p.sbuf("sq", [128, 3, TT], F32)
    lnbf = p.sbuf("lnbf", [128, NK, T], BF16)
    onT = p.sbuf("onT", [128, NK, TT], BF16)
    otile = p.sbuf("otile", [128, D], F32)
    rr = p.sbuf("rr", [128, 4], F32)
    onbf = p.sbuf("onbf", [128, D], BF16)
    rw_sb = p.sbuf("rw_sb", [128, D], F32)
    idb_sb = p.sbuf("idb_sb", [128, 128], BF16)
    wo_sb = p.sbuf("wo_sb", [128, NK, D], BF16)
    work = p.sbuf("work", [1, 3 * T], F32)
    ones_col = p.sbuf("ones_col", [128, 1], F32)
    ones_row = p.sbuf("ones_row", [1, 128], F32)
    alloc_ln(p)
    ps_sx = p.psum("ps_sx", [1, TT], F32)
    ps_sxx = p.psum("ps_sxx", [1, TT], F32)
    ps_bc = p.psum("ps_bc", [128, 2 * TT], F32)
    ps_a = p.psum("ps_a", [128, TT], F32)
    ps_b = ps_a
    ps_t = p.psum("ps_t", [128, 128], BF16)

    p.sec('vector', lambda v: [v.memset(ones_col[:, :], 1.0),
                               v.memset(ones_row[:, :], 1.0)])
    p.dma_sec_n('sync', 2, lambda e: [e.dma_start(out=rw_sb[:, :], in_=rw[:, :]),
                                      e.dma_start(out=idb_sb[:, :], in_=idb[:, :])])
    p.dma_sec_n('gpsimd', 1, lambda e: [
        e.dma_start(out=wo_sb[:, :, :], in_=wo.rearrange("(k p) o -> p k o", p=128))])

    p.dma_sec_n('sync', NK, lambda e: [
        e.dma_start(out=xs[:, k, :], in_=x1[k * 128:(k + 1) * 128, :])
        for k in range(NK)])
    for tt_i in range(NT):
        tsl = slice(tt_i * TT, (tt_i + 1) * TT)
        for c4 in range(4):
            ci = tt_i * 4 + c4
            p.dma_sec_n('sync', 1, lambda e, ci=ci: [
                e.dma_start(out=otile[:, :], in_=of[ci * 128:(ci + 1) * 128, :])])
            p.sec('scalar', lambda s: s.square(ps_bc[:, 0:D], otile[:, :]))

            def v_m(v):
                v.reduce_sum(rr[:, 0:1], ps_bc[:, 0:D], AX.X)
                v.tensor_scalar(rr[:, 1:2], rr[:, 0:1], 1.0 / D, 1e-6,
                                ALU.mult, ALU.add)
                return v.tensor_scalar_mul(rr[:, 1:2], rr[:, 1:2], 2.0 ** -64)
            p.sec('vector', v_m)
            p.sec('scalar', lambda s: s.activation(rr[:, 2:3], rr[:, 1:2], AF.Sqrt))

            def v_n(v):
                v.reciprocal(rr[:, 0:1], rr[:, 2:3])
                v.tensor_scalar_mul(rr[:, 0:1], rr[:, 0:1], 2.0 ** -32)
                v.tensor_scalar_mul(otile[:, :], otile[:, :], rr[:, 0:1])
                return v.tensor_tensor(onbf[:, :], otile[:, :], rw_sb[:, :], ALU.mult)
            p.sec('vector', v_n)
            for k in range(NK):
                p.sec('tensor', lambda te, k=k: te.transpose(
                    ps_t[:, :], onbf[:, k * 128:(k + 1) * 128], idb_sb[:, :]))
                p.sec('vector', lambda v, k=k, c4=c4: v.tensor_copy(
                    onT[:, k, c4 * 128:(c4 + 1) * 128], ps_t[:, :]))
        for oc in range(NK):
            def t_mm(te, oc=oc):
                for k in range(NK):
                    ins = te.matmul(ps_a[:, :], wo_sb[:, k, oc * 128:(oc + 1) * 128],
                                    onT[:, k, :], start=(k == 0), stop=(k == NK - 1))
                return ins
            p.sec('tensor', t_mm)
            p.sec('vector', lambda v, oc=oc, tsl=tsl: v.tensor_tensor(
                xs[:, oc, tsl], xs[:, oc, tsl], ps_a[:, :], ALU.add))
    xo = nc.dram_tensor("x_out", [D, T], F32, kind="ExternalOutput").ap()
    p.dma_sec_n('sync', NK, lambda e: [
        e.dma_start(out=xo[k * 128:(k + 1) * 128, :], in_=xs[:, k, :])
        for k in range(NK)])

    emit_ln(p, xs, lnbf, sq, work, ps_sx, ps_sxx, ps_bc, ones_col, ones_row)
    gw = nc.dram_tensor("gwT", [D, 8], F32, kind="ExternalInput").ap()
    gb = nc.dram_tensor("gb", [8], F32, kind="ExternalInput").ap()
    idf = nc.dram_tensor("identf", [128, 128], F32, kind="ExternalInput").ap()
    gates_out = nc.dram_tensor("gates", [8, T], F32, kind="ExternalOutput").ap()
    lnout_hbm = nc.dram_tensor("lnbf_out", [D, T], BF16, kind="ExternalOutput").ap()
    gw_sb = p.sbuf("gw_sb", [128, NK, 8], BF16)
    idf_sb = p.sbuf("idf_sb", [128, 128], F32)
    gfm = p.sbuf("gfm", [8, T], F32)
    ptm = p.sbuf("ptm", [128, 16, 26], F32)
    negt = p.sbuf("negt", [128, 8], F32)
    gb_sb = p.sbuf("gb_sb", [8, 1], F32)
    p.dma_sec_n('sync', 2, lambda e: [e.dma_start(out=idf_sb[:, :], in_=idf[:, :]),
                                      e.dma_start(out=gb_sb[:, 0], in_=gb[:])])
    p.dma_sec_n('gpsimd', 1, lambda e: [
        e.dma_start(out=gw_sb[:, :, :], in_=gw.rearrange("(k p) e -> p k e", p=128))])
    p.sec('vector', lambda v: v.memset(negt[:, :], -1e30))
    emit_logits(p, lnbf, gw_sb, 8, ps_a, gfm, bias_sb=gb_sb)
    emit_softmax_gates(p, gfm, gfm, ptm, ps_bc, idf_sb, negt, 'top2')
    p.dma_sec_n('sync', 1 + NK, lambda e: [
        e.dma_start(out=gates_out[:, :], in_=gfm[:, :])] + [
        e.dma_start(out=lnout_hbm[k * 128:(k + 1) * 128, :], in_=lnbf[:, k, :])
        for k in range(NK)])
    p.fin()
    return p.close()


def build_route():
    p = Prog("route").start()
    nc = p.nc
    pr = nc.dram_tensor("p_resh", [128, 1024], F32, kind="ExternalInput").ap()
    po = nc.dram_tensor("probs_own", [8, T], F32, kind="ExternalInput").ap()
    go = nc.dram_tensor("groupones", [128, 8], F32, kind="ExternalInput").ap()
    sg = nc.dram_tensor("selg", [8, 128], F32, kind="ExternalInput").ap()
    pk = nc.dram_tensor("pick", [128, 8], F32, kind="ExternalInput").ap()
    idf = nc.dram_tensor("identf", [128, 128], F32, kind="ExternalInput").ap()
    gout = nc.dram_tensor("gates", [8, T], F32, kind="ExternalOutput").ap()

    prs = p.sbuf("prs", [128, 1024], F32)
    pos = p.sbuf("pos", [8, T], F32)
    gos = p.sbuf("gos", [128, 8], F32)
    sgs = p.sbuf("sgs", [8, 128], F32)
    pks = p.sbuf("pks", [128, 8], F32)
    idfs = p.sbuf("idfs", [128, 128], F32)
    lo = p.sbuf("lo", [128, 1], F32)
    hi = p.sbuf("hi", [128, 1], F32)
    mid = p.sbuf("mid", [128, 3], F32)
    cmp_t = p.sbuf("cmp_t", [128, 1024], F32)
    c8 = p.sbuf("c8", [8, 2], F32)
    th = p.sbuf("th", [128, 8], F32)
    ptm = p.sbuf("ptm", [128, 16, 26], F32)
    negt = p.sbuf("negt", [128, 8], F32)
    gfm = p.sbuf("gfm", [8, T], F32)
    ps_r = p.psum("ps_r", [128, 128], F32)

    p.dma_sec_n('sync', 6, lambda e: [
        e.dma_start(out=prs[:, :], in_=pr[:, :]),
        e.dma_start(out=pos[:, :], in_=po[:, :]),
        e.dma_start(out=gos[:, :], in_=go[:, :]),
        e.dma_start(out=sgs[:, :], in_=sg[:, :]),
        e.dma_start(out=pks[:, :], in_=pk[:, :]),
        e.dma_start(out=idfs[:, :], in_=idf[:, :])])
    p.sec('vector', lambda v: [v.memset(lo[:, :], 0.0), v.memset(hi[:, :], 1.0),
                               v.memset(negt[:, :], -1e30)])

    CAP = float(T * 8)
    for it in range(34):
        def v_mid(v):
            v.tensor_tensor(mid[:, 0:1], lo[:, :], hi[:, :], ALU.add)
            v.tensor_scalar_mul(mid[:, 0:1], mid[:, 0:1], 0.5)
            v.tensor_scalar(cmp_t[:, :], prs[:, :], mid[:, 0:1], None, ALU.is_gt)
            return v.reduce_sum(mid[:, 1:2], cmp_t[:, :], AX.X)
        p.sec('vector', v_mid)
        p.sec('tensor', lambda te: te.matmul(ps_r[0:8, 0:1], gos[:, :], mid[:, 1:2],
                                             start=True, stop=True))
        p.sec('vector', lambda v: v.tensor_copy(c8[:, 0:1], ps_r[0:8, 0:1]))
        p.sec('tensor', lambda te: te.matmul(ps_r[:, 0:1], sgs[:, :], c8[:, 0:1],
                                             start=True, stop=True))

        def v_upd(v):
            v.tensor_copy(mid[:, 1:2], ps_r[:, 0:1])
            v.tensor_scalar(mid[:, 2:3], mid[:, 1:2], 2048.0, None, ALU.is_ge)
            v.tensor_tensor(mid[:, 1:2], mid[:, 0:1], lo[:, :], ALU.subtract)
            v.tensor_tensor(mid[:, 1:2], mid[:, 2:3], mid[:, 1:2], ALU.mult)
            v.tensor_tensor(lo[:, :], lo[:, :], mid[:, 1:2], ALU.add)
            v.tensor_tensor(mid[:, 1:2], mid[:, 0:1], hi[:, :], ALU.subtract)
            v.tensor_tensor(hi[:, :], hi[:, :], mid[:, 1:2], ALU.add)
            v.tensor_tensor(mid[:, 1:2], mid[:, 2:3], mid[:, 1:2], ALU.mult)
            return v.tensor_tensor(hi[:, :], hi[:, :], mid[:, 1:2], ALU.subtract)
        p.sec('vector', v_upd)

    p.sec('tensor', lambda te: te.matmul(ps_r[0:1, 0:8], lo[:, :], pks[:, :],
                                         start=True, stop=True))
    p.sec('vector', lambda v: v.tensor_copy(c8[0:1, 0:2].bitcast(F32), ps_r[0:1, 0:2]))
    p.sec('vector', lambda v: v.tensor_copy(th[0:1, 0:8], ps_r[0:1, 0:8]))
    ones_row = p.sbuf("ones_row", [1, 128], F32)
    p.sec('vector', lambda v: v.memset(ones_row[:, :], 1.0))
    p.sec('tensor', lambda te: te.matmul(ps_r[:, 0:8], ones_row[0:1, :], th[0:1, 0:8],
                                         start=True, stop=True))
    p.sec('vector', lambda v: v.tensor_copy(th[:, :], ps_r[:, 0:8]))

    for i in range(16):
        p.sec('tensor', lambda te, i=i: te.transpose(
            ps_r[:, 0:8], pos[0:8, i * 128:(i + 1) * 128], idfs[0:8, 0:8]))
        p.sec('vector', lambda v, i=i: v.tensor_copy(ptm[:, i, 0:8], ps_r[:, 0:8]))

    def v_gate(v):
        out = []
        for i in range(16):
            tile = ptm[:, i, 0:8]
            msk = ptm[:, i, 8:16]
            mskd = ptm[:, i, 16:24]
            a1 = ptm[:, i, 24:25]
            a2 = ptm[:, i, 25:26]
            v.tensor_tensor(msk, tile, th[:, :], ALU.is_gt)
            v.tensor_tensor(mskd, msk, tile, ALU.mult)
            v.tensor_scalar(msk, msk, 1e30, -1e30, ALU.mult, ALU.add)
            v.tensor_tensor(mskd, mskd, msk, ALU.add)
            v.reduce_max(a1, mskd, AX.X)
            v.reduce_max(a2, tile, AX.X)
            v.tensor_scalar(msk[:, 0:1], a1, -1e29, None, ALU.is_gt)
            v.tensor_tensor(mskd[:, 0:1], a1, a2, ALU.subtract)
            v.tensor_tensor(mskd[:, 0:1], msk[:, 0:1], mskd[:, 0:1], ALU.mult)
            v.tensor_tensor(a2, a2, mskd[:, 0:1], ALU.add)
            v.tensor_scalar(msk, tile, a2, None, ALU.is_equal)
            out.append(v.tensor_scalar(tile, msk, a2, None, ALU.mult))
        return out
    p.sec('vector', v_gate)

    for i in range(16):
        p.sec('tensor', lambda te, i=i: te.transpose(
            ps_r[0:8, :], ptm[:, i, 0:8], idfs[:, :]))
        p.sec('vector', lambda v, i=i: v.tensor_copy(
            gfm[0:8, i * 128:(i + 1) * 128], ps_r[0:8, 0:128]))
    p.dma_sec_n('sync', 1, lambda e: [e.dma_start(out=gout[:, :], in_=gfm[:, :])])
    p.fin()
    return p.close()


def build_attn1():
    p = Prog("attn").start()
    nc = p.nc
    q4 = nc.dram_tensor("q4", [D, T], BF16, kind="ExternalInput").ap()
    k4 = nc.dram_tensor("k4", [D, T], BF16, kind="ExternalInput").ap()
    v4 = nc.dram_tensor("v4", [D, T], BF16, kind="ExternalInput").ap()
    idb = nc.dram_tensor("identb", [128, 128], BF16, kind="ExternalInput").ap()
    ao = nc.dram_tensor("attnout", [D, T], BF16, kind="ExternalOutput").ap()

    qs = p.sbuf("qs", [128, NK, T], BF16)
    ks = p.sbuf("ks", [128, NK, T], BF16)
    vs = p.sbuf("vs", [128, NK, T], BF16)
    vtm = p.sbuf("vtm", [128, 16, HD], BF16)
    wT = p.sbuf("wT", [128, 16, TT], BF16)
    rs = p.sbuf("rs", [1, TT], F32)
    rb_sb = p.sbuf("rb_sb", [64, TT], F32)
    aos = p.sbuf("aos", [128, NK, T], BF16)
    idb_sb = p.sbuf("idb_sb", [128, 128], BF16)
    onesb = p.sbuf("onesb", [128, 1], BF16)
    ones_row = p.sbuf("ones_row", [1, 128], F32)
    ps8 = p.psum("ps8", [128, 8, TT], F32)

    p.dma_sec_n('sync', 1 + 3 * NK, lambda e: [
        e.dma_start(out=idb_sb[:, :], in_=idb[:, :])] + [
        e.dma_start(out=t[:, k, :], in_=a[k * 128:(k + 1) * 128, :])
        for t, a in ((qs, q4), (ks, k4), (vs, v4)) for k in range(NK)])
    p.sec('vector', lambda v: [v.memset(onesb[:, :], 1.0),
                               v.memset(ones_row[:, :], 1.0)])

    for h in range(NHEAD):
        half, oc = (h % 2) * 64, h // 2
        for r in range(2):
            def t_tr(te, r=r, half=half, oc=oc):
                out = []
                for j in range(8):
                    kc = r * 8 + j
                    out.append(te.transpose(
                        ps8[:, j, 0:32].bitcast(BF16),
                        vs[half:half + 64, oc, kc * 128:(kc + 1) * 128],
                        idb_sb[half:half + 64, half:half + 64]))
                return out
            p.sec('tensor', t_tr)

            def v_tr(v, r=r):
                return [v.tensor_copy(vtm[:, r * 8 + j, :],
                                      ps8[:, j, 0:32].bitcast(BF16))
                        for j in range(8)]
            p.sec('vector', v_tr)

        for qt in range(NT):
            qsl = slice(qt * TT, (qt + 1) * TT)
            for r in range(2):
                def t_s(te, r=r, half=half, oc=oc, qsl=qsl):
                    out = []
                    for j in range(8):
                        kc = r * 8 + j
                        out.append(te.matmul(
                            ps8[:, j, :], ks[half:half + 64, oc, kc * 128:(kc + 1) * 128],
                            qs[half:half + 64, oc, qsl], start=True, stop=True))
                    return out
                p.sec('tensor', t_s)

                def s_s(s, r=r):
                    return [s.activation(wT[:, r * 8 + j, :], ps8[:, j, :],
                                         AF.Relu, scale=0.125) for j in range(8)]
                p.sec('scalar', s_s)
            def t_r(te):
                for kc in range(16):
                    ins = te.matmul(ps8[0:1, 0, :], onesb[:, :], wT[:, kc, :],
                                    start=(kc == 0), stop=(kc == 15))
                return ins
            p.sec('tensor', t_r)

            def v_r(v):
                v.tensor_copy(rs[:, :], ps8[0:1, 0, :])
                v.tensor_scalar_add(rs[:, :], rs[:, :], 1e-6)
                return v.reciprocal(rs[:, :], rs[:, :])
            p.sec('vector', v_r)
            p.sec('tensor', lambda te: te.matmul(ps8[0:64, 1, :], ones_row[0:1, 0:64],
                                                 rs[0:1, :], start=True, stop=True))
            p.sec('vector', lambda v: v.tensor_copy(rb_sb[:, :], ps8[0:64, 1, :]))

            def t_o(te):
                for kc in range(16):
                    ins = te.matmul(ps8[0:64, 2, :], vtm[:, kc, :], wT[:, kc, :],
                                    start=(kc == 0), stop=(kc == 15))
                return ins
            p.sec('tensor', t_o)
            p.sec('vector', lambda v, half=half, oc=oc, qsl=qsl: v.tensor_tensor(
                aos[half:half + 64, oc, qsl], ps8[0:64, 2, :], rb_sb[:, :], ALU.mult))

    p.dma_sec_n('sync', NK, lambda e: [
        e.dma_start(out=ao[k * 128:(k + 1) * 128, :], in_=aos[:, k, :])
        for k in range(NK)])
    p.fin()
    return p.close()


def build_attnout():
    p = Prog("aout").start()
    nc = p.nc
    at = nc.dram_tensor("attnout", [D, T], BF16, kind="ExternalInput").ap()
    xl = nc.dram_tensor("xln4", [D, T], F32, kind="ExternalInput").ap()
    x4 = nc.dram_tensor("xT", [D, T], F32, kind="ExternalInput").ap()
    wo = nc.dram_tensor("w_out", [D, D], F32, kind="ExternalInput").ap()
    bo = nc.dram_tensor("b_out_r", [128, 8], F32, kind="ExternalInput").ap()
    ga = nc.dram_tensor("gamma_r", [128, 8], F32, kind="ExternalInput").ap()
    be = nc.dram_tensor("beta_r", [128, 8], F32, kind="ExternalInput").ap()
    x5o = nc.dram_tensor("x_out", [D, T], F32, kind="ExternalOutput").ap()

    xs = p.sbuf("xs", [128, NK, T], F32)
    sq = p.sbuf("sq", [128, 3, TT], F32)
    zt = p.sbuf("zt", [128, T], F32)
    ats = p.sbuf("ats", [128, NK, TT], BF16)
    wos = p.sbuf("wos", [128, NK, D], BF16)
    bos = p.sbuf("bos", [128, 8], F32)
    gas = p.sbuf("gas", [128, 8], F32)
    bes = p.sbuf("bes", [128, 8], F32)
    x4t = p.sbuf("x4t", [128, T], F32)
    work = p.sbuf("work", [1, 3 * T], F32)
    ones_col = p.sbuf("ones_col", [128, 1], F32)
    ones_row = p.sbuf("ones_row", [1, 128], F32)
    alloc_ln(p)
    ps_sx = p.psum("ps_sx", [1, TT], F32)
    ps_sxx = p.psum("ps_sxx", [1, TT], F32)
    ps_bc = p.psum("ps_bc", [128, 2 * TT], F32)
    ps_a = p.psum("ps_a", [128, TT], F32)

    p.sec('vector', lambda v: [v.memset(ones_col[:, :], 1.0),
                               v.memset(ones_row[:, :], 1.0)])
    p.dma_sec_n('sync', 3 + NK, lambda e: [
        e.dma_start(out=bos[:, :], in_=bo[:, :]),
        e.dma_start(out=gas[:, :], in_=ga[:, :]),
        e.dma_start(out=bes[:, :], in_=be[:, :])] + [
        e.dma_start(out=xs[:, k, :], in_=xl[k * 128:(k + 1) * 128, :])
        for k in range(NK)])
    p.dma_sec_n('gpsimd', 1, lambda e: [
        e.dma_start(out=wos[:, :, :], in_=wo.rearrange("(k p) o -> p k o", p=128))])

    for tt_i in range(NT):
        tsl = slice(tt_i * TT, (tt_i + 1) * TT)
        p.dma_sec_n('sync', 1, lambda e, tsl=tsl: [
            e.dma_start(out=ats[:, :, :],
                        in_=at[:, tsl].rearrange("(k p) c -> p k c", p=128))])
        for oc in range(NK):
            def t_mm(te, oc=oc):
                for k in range(NK):
                    ins = te.matmul(ps_a[:, :], wos[:, k, oc * 128:(oc + 1) * 128],
                                    ats[:, k, :], start=(k == 0), stop=(k == NK - 1))
                return ins
            p.sec('tensor', t_mm)

            def v_y(v, oc=oc, tsl=tsl):
                v.tensor_scalar(sq[:, 0, 0:TT], ps_a[:, :], bos[:, oc:oc + 1],
                                None, ALU.add)
                return v.tensor_tensor(xs[:, oc, tsl], xs[:, oc, tsl],
                                       sq[:, 0, 0:TT], ALU.add)
            p.sec('vector', v_y)

    lnf = p.sbuf("lnf", [128, NK, T], BF16)
    emit_ln(p, xs, lnf, sq, work, ps_sx, ps_sxx, ps_bc, ones_col, ones_row)

    for k in range(NK):
        p.dma_sec_n('sync', 1, lambda e, k=k: [
            e.dma_start(out=x4t[:, :], in_=x4[k * 128:(k + 1) * 128, :])])

        def v_z(v, k=k):
            v.tensor_scalar(zt[:, :], lnf[:, k, :], gas[:, k:k + 1], None, ALU.mult)
            v.tensor_scalar(zt[:, :], zt[:, :], bes[:, k:k + 1], None, ALU.add)
            return v.tensor_tensor(xs[:, k, :], x4t[:, :], zt[:, :], ALU.add)
        p.sec('vector', v_z)
        p.dma_sec_n('sync', 1, lambda e, k=k: [
            e.dma_start(out=x5o[k * 128:(k + 1) * 128, :], in_=xs[:, k, :])])

    emit_ln(p, xs, lnf, sq, work, ps_sx, ps_sxx, ps_bc, ones_col, ones_row)
    gw = nc.dram_tensor("gwT", [D, 8], F32, kind="ExternalInput").ap()
    idf = nc.dram_tensor("identf", [128, 128], F32, kind="ExternalInput").ap()
    gates_out = nc.dram_tensor("gates", [8, T], F32, kind="ExternalOutput").ap()
    lnout_hbm = nc.dram_tensor("lnbf_out", [D, T], BF16, kind="ExternalOutput").ap()
    gw_sb = p.sbuf("gw_sb", [128, NK, 8], BF16)
    idf_sb = p.sbuf("idf_sb", [128, 128], F32)
    gfm = p.sbuf("gfm", [8, T], F32)
    ptm = p.sbuf("ptm", [128, 16, 26], F32)
    negt = p.sbuf("negt", [128, 8], F32)
    p.dma_sec_n('sync', 1, lambda e: [e.dma_start(out=idf_sb[:, :], in_=idf[:, :])])
    p.dma_sec_n('gpsimd', 1, lambda e: [
        e.dma_start(out=gw_sb[:, :, :], in_=gw.rearrange("(k p) e -> p k e", p=128))])
    p.sec('vector', lambda v: v.memset(negt[:, :], -1e30))
    emit_logits(p, lnf, gw_sb, 8, ps_a, gfm, bias_sb=None)
    emit_softmax_gates(p, gfm, gfm, ptm, ps_bc, idf_sb, negt, 'probs')
    p.dma_sec_n('sync', 1 + NK, lambda e: [
        e.dma_start(out=gates_out[:, :], in_=gfm[:, :])] + [
        e.dma_start(out=lnout_hbm[k * 128:(k + 1) * 128, :], in_=lnf[:, k, :])
        for k in range(NK)])
    p.fin()
    return p.close()


def build_final():
    p = Prog("final").start()
    nc = p.nc
    xT = nc.dram_tensor("xT", [D, T], F32, kind="ExternalInput").ap()
    pp = nc.dram_tensor("pparts", [8, D, T], F32, kind="ExternalInput").ap()
    xo = nc.dram_tensor("x_out", [D, T], F32, kind="ExternalOutput").ap()
    xt = p.sbuf("xt", [128, T], F32)
    pt = p.sbuf("pt", [128, 2, T], F32)
    acc = p.sbuf("acc", [128, T], F32)
    for k in range(NK):
        p.dma_sec_n('sync', 1, lambda e, k=k: [
            e.dma_start(out=xt[:, :], in_=xT[k * 128:(k + 1) * 128, :])])
        for j0 in range(0, 8, 2):
            p.dma_sec_n('sync', 2, lambda e, k=k, j0=j0: [
                e.dma_start(out=pt[:, j % 2, :], in_=pp[j, k * 128:(k + 1) * 128, :])
                for j in range(j0, j0 + 2)])

            def v_acc(v, j0=j0):
                out = []
                for j in range(j0, j0 + 2):
                    if j == 0:
                        out.append(v.tensor_copy(acc[:, :], pt[:, 0, :]))
                    else:
                        out.append(v.tensor_tensor(acc[:, :], acc[:, :],
                                                   pt[:, j % 2, :], ALU.add))
                return out
            p.sec('vector', v_acc)
        p.sec('vector', lambda v: v.scalar_tensor_tensor(
            xt[:, :], acc[:, :], 0.5, xt[:, :], ALU.mult, ALU.add))
        p.dma_sec_n('sync', 1, lambda e, k=k: [
            e.dma_start(out=xo[k * 128:(k + 1) * 128, :], in_=xt[:, :])])
    p.fin()
    return p.close()


_PROGS = {}


def _get(name, fn):
    if name not in _PROGS:
        _PROGS[name] = fn()
    return _PROGS[name]


def _run(nc, maps):
    res = run_bass_kernel_spmd(nc, maps, core_ids=list(range(8)))
    return res.results


def _consts():
    c = {}
    c['identf'] = np.eye(128, dtype=np.float32)
    c['identb'] = np.eye(128, dtype=np.float32).astype(BF)
    bd2 = np.zeros((128, 2), np.float32)
    bd2[0:64, 0] = 1.0
    bd2[64:128, 1] = 1.0
    c['blockdiag2'] = bd2
    bc = np.zeros((2, 128), np.float32)
    bc[0, 0:64] = 1.0
    bc[1, 64:128] = 1.0
    c['bcast64'] = bc
    sel8 = np.zeros((8, 1024), np.float32)
    for e in range(8):
        sel8[e, e * 128:(e + 1) * 128] = 1.0
    c['sel8'] = sel8
    go = np.zeros((128, 8), np.float32)
    sg = np.zeros((8, 128), np.float32)
    pk = np.zeros((128, 8), np.float32)
    for e in range(8):
        go[16 * e:16 * (e + 1), e] = 1.0
        sg[e, 16 * e:16 * (e + 1)] = 1.0
        pk[16 * e, e] = 1.0
    c['groupones'] = go
    c['selg'] = sg
    c['pick'] = pk
    return c


def kernel(**inp):
    f = lambda a: np.ascontiguousarray(np.asarray(a, dtype=np.float32))
    x = f(inp['L0_attn_x'])
    C = _consts()
    N = 8

    xT = [np.ascontiguousarray(x[i].T) for i in range(N)]

    gwT = f(inp['L0_ffn_gate_w'].T)
    gb = f(inp['L0_ffn_gate_b'])
    w1 = f(inp['L0_ffn_w1'])
    w2 = f(inp['L0_ffn_w2'])
    b1r = np.ascontiguousarray(f(inp['L0_ffn_b1']).reshape(8, 32, 128).transpose(0, 2, 1))
    b2r = np.ascontiguousarray(f(inp['L0_ffn_b2']).reshape(8, 8, 128).transpose(0, 2, 1))
    WqT = f(inp['L0_attn_Wq'].T)
    WkT = f(inp['L0_attn_Wk'].T)
    WvT = f(inp['L0_attn_Wv'].T)
    WoT = f(inp['L0_attn_Wo'].T)
    WbetaT = f(inp['L0_attn_Wbeta'].T)
    rmsw_rep = np.ascontiguousarray(np.tile(f(inp['L0_attn_rms_weight'])[None, :], (128, 1)))
    Wg = f(inp['L1_ffn_Wg'])
    W1 = f(inp['L1_ffn_W1'])
    W2 = f(inp['L1_ffn_W2'])
    perm = np.array([h * 192 + o * 64 + d for o in range(3) for h in range(16)
                     for d in range(64)])
    wqkvP = np.ascontiguousarray(f(inp['L1_attn_w_qkv'])[:, perm])
    w_out = f(inp['L1_attn_w_out'])
    b_out_r = np.ascontiguousarray(f(inp['L1_attn_b_out']).reshape(8, 128).T)
    gamma_r = np.ascontiguousarray(f(inp['L1_attn_gamma']).reshape(8, 128).T)
    beta_r = np.ascontiguousarray(f(inp['L1_attn_beta']).reshape(8, 128).T)

    P_pre = _get('pre', lambda: build_stage(0, {'gates0'}))
    P_expL0 = _get('expL0', lambda: build_experts('L0', 2))
    P_comb_proj = _get('comb_proj', lambda: build_stage(8, {'out_x', 'projL0'}))
    P_scan = _get('scan', build_scan)
    P_rmswo = _get('rmswo', build_rmswo)
    P_comb_probs = _get('comb_probs', lambda: build_stage(8, {'out_x', 'probs1'}))
    P_route = _get('route', build_route)
    P_expL1 = _get('expL1', lambda: build_experts('L1', 4))
    P_comb_qkv = _get('comb_qkv', lambda: build_stage(8, {'out_x', 'qkv1'}))
    P_attn = _get('attn', build_attn1)
    P_aout = _get('aout', build_attnout)
    P_final = _get('final', build_final)

    def ffn0(lnb, gts):
        parts = [None] * N
        for g in range(4):
            r = _run(P_expL0, [{
                'lnbf_in': lnb[i], 'gates': gts[i],
                'sel2': np.ascontiguousarray(C['sel8'][:, g * 256:(g + 1) * 256]),
                'w1': w1[2 * g:2 * g + 2], 'w2': w2[2 * g:2 * g + 2],
                'b1': b1r[2 * g:2 * g + 2], 'b2': b2r[2 * g:2 * g + 2],
            } for i in range(N)])
            for i in range(N):
                pi = r[i]['partials']
                parts[i] = pi if g == 0 else np.concatenate([parts[i], pi], 0)
        return parts

    def ffn1(lnb, gts):
        parts = [None] * N
        for g in range(2):
            r = _run(P_expL1, [{
                'lnbf_in': lnb[i], 'gates': gts[i],
                'sel2': np.ascontiguousarray(C['sel8'][:, g * 512:(g + 1) * 512]),
                'w1': W1[4 * g:4 * g + 4], 'w2': W2[4 * g:4 * g + 4],
            } for i in range(N)])
            for i in range(N):
                pi = r[i]['partials']
                parts[i] = pi if g == 0 else np.concatenate([parts[i], pi], 0)
        return parts

    def route(probs):
        pall = np.concatenate(probs, axis=1)
        p_resh = np.ascontiguousarray(pall.reshape(8, 16, 1024).reshape(128, 1024))
        r = _run(P_route, [{
            'p_resh': p_resh, 'probs_own': probs[i], 'groupones': C['groupones'],
            'selg': C['selg'], 'pick': C['pick'], 'identf': C['identf'],
        } for i in range(N)])
        return [r[i]['gates'] for i in range(N)]

    r = _run(P_pre, [{'xT': xT[i], 'gwT': gwT, 'gb': gb, 'identf': C['identf']}
                     for i in range(N)])
    ln0 = [r[i]['lnbf_out'] for i in range(N)]
    gates0 = [r[i]['gates'] for i in range(N)]

    parts = ffn0(ln0, gates0)

    r = _run(P_comb_proj, [{
        'xT': xT[i], 'pparts': parts[i], 'WqT': WqT, 'WkT': WkT, 'WvT': WvT,
        'WbetaT': WbetaT, 'blockdiag2': C['blockdiag2'], 'bcast64': C['bcast64'],
        'identf': C['identf'],
    } for i in range(N)])
    x1 = [r[i]['x_out'] for i in range(N)]
    qlm = [r[i]['q_lm'] for i in range(N)]
    klm = [r[i]['k_lm'] for i in range(N)]
    vlm = [r[i]['v_lm'] for i in range(N)]
    beta = [r[i]['beta'] for i in range(N)]

    M = [np.zeros((16, 4096), np.float32)] * N
    oparts = [[] for _ in range(N)]
    for part in range(4):
        sl = slice(part * 512, (part + 1) * 512)
        r = _run(P_scan, [{
            'qp': np.ascontiguousarray(qlm[i][:, sl, :]),
            'kp': np.ascontiguousarray(klm[i][:, sl, :]),
            'vp': np.ascontiguousarray(vlm[i][:, sl, :]),
            'betap': np.ascontiguousarray(beta[i][:, sl]),
            'M_in': M[i],
        } for i in range(N)])
        M = [r[i]['M_out'] for i in range(N)]
        for i in range(N):
            oparts[i].append(r[i]['o_part'])

    r = _run(P_rmswo, [{
        'o_full': np.ascontiguousarray(np.concatenate(oparts[i], 0)),
        'xT': x1[i], 'WoT': WoT, 'rmsw_rep': rmsw_rep, 'identb': C['identb'],
        'gwT': gwT, 'gb': gb, 'identf': C['identf'],
    } for i in range(N)])
    x2 = [r[i]['x_out'] for i in range(N)]
    ln2 = [r[i]['lnbf_out'] for i in range(N)]
    gates2 = [r[i]['gates'] for i in range(N)]

    parts = ffn0(ln2, gates2)

    r = _run(P_comb_probs, [{'xT': x2[i], 'pparts': parts[i], 'gwT': Wg,
                             'identf': C['identf']} for i in range(N)])
    x3 = [r[i]['x_out'] for i in range(N)]
    ln3 = [r[i]['lnbf_out'] for i in range(N)]
    probs3 = [r[i]['gates'] for i in range(N)]

    gates3 = route(probs3)
    parts = ffn1(ln3, gates3)

    r = _run(P_comb_qkv, [{'xT': x3[i], 'pparts': parts[i], 'wqkvP': wqkvP}
                          for i in range(N)])
    x4 = [r[i]['x_out'] for i in range(N)]
    xln4 = [r[i]['xln_out'] for i in range(N)]
    q4 = [r[i]['q4'] for i in range(N)]
    k4 = [r[i]['k4'] for i in range(N)]
    v4 = [r[i]['v4'] for i in range(N)]

    r = _run(P_attn, [{'q4': q4[i], 'k4': k4[i], 'v4': v4[i],
                       'identb': C['identb']} for i in range(N)])
    ao = [r[i]['attnout'] for i in range(N)]
    r = _run(P_aout, [{'attnout': ao[i], 'xln4': xln4[i], 'xT': x4[i],
                       'w_out': w_out, 'b_out_r': b_out_r, 'gamma_r': gamma_r,
                       'beta_r': beta_r, 'gwT': Wg, 'identf': C['identf']}
                      for i in range(N)])
    x5 = [r[i]['x_out'] for i in range(N)]
    ln5 = [r[i]['lnbf_out'] for i in range(N)]
    probs5 = [r[i]['gates'] for i in range(N)]

    gates5 = route(probs5)
    parts = ffn1(ln5, gates5)

    r = _run(P_final, [{'xT': x5[i], 'pparts': parts[i]} for i in range(N)])
    out = np.stack([np.asarray(r[i]['x_out']).T for i in range(N)], axis=0)
    return np.ascontiguousarray(out.astype(np.float32))


# version 05 (final)
# speedup vs baseline: 1.1405x; 1.1405x over previous
import contextlib
import os
import numpy as np
try:
    import jax
    os.makedirs("/tmp/jaxcache", exist_ok=True)
    jax.config.update("jax_compilation_cache_dir", "/tmp/jaxcache")
    jax.config.update("jax_persistent_cache_min_compile_time_secs", 0.0)
    jax.config.update("jax_persistent_cache_min_entry_size_bytes", 0)
except Exception:
    pass
import ml_dtypes
import concourse.bass as bass
import concourse.mybir as mybir
from concourse.bass_utils import run_bass_kernel_spmd

F32 = mybir.dt.float32
BF16 = mybir.dt.bfloat16
AF = mybir.ActivationFunctionType
ALU = mybir.AluOpType
AX = mybir.AxisListType

D = 1024
T = 2048
NK = 8
NT = 4
TT = 512
NHEAD = 16
HD = 64
BF = ml_dtypes.bfloat16
LN_EPS = 1e-5


class Prog:
    def __init__(self, name):
        self.nc = bass.Bass("TRN2", target_bir_lowering=False)
        self.stack = contextlib.ExitStack()
        self.blk = None
        self.flow = None
        self.dflow = None
        self.sflow = None
        self.val = 0
        self.dval = 0
        self.sval = 0

    def start(self):
        self.flow = self.stack.enter_context(self.nc.semaphore("flow"))
        self.dflow = self.stack.enter_context(self.nc.semaphore("dflow"))
        self.sflow = self.stack.enter_context(self.nc.semaphore("sflow"))
        self._chain = self.stack.enter_context(self.nc.semaphore("chain"))
        self._chain_n = 0
        self.blk = self.stack.enter_context(self.nc.Block())
        return self

    def sbuf(self, name, shape, dtype):
        return self.stack.enter_context(self.nc.sbuf_tensor(name, shape, dtype))

    def psum(self, name, shape, dtype=F32):
        return self.stack.enter_context(self.nc.psum_tensor(name, shape, dtype))

    def sem(self, name):
        return self.stack.enter_context(self.nc.semaphore(name))

    def sec(self, eng, fn):
        v, dv, sv = self.val, self.dval, self.sval
        spaced = eng in ('vector', 'scalar', 'gpsimd')

        def body(e):
            if v > 0:
                e.wait_ge(self.flow, v)
            if dv > 0:
                e.wait_ge(self.dflow, dv)
            if sv > 0:
                e.wait_ge(self.sflow, sv)
            ee = _Spaced(e, self) if spaced else e
            last = fn(ee)
            if isinstance(last, list):
                last = last[-1]
            last.then_inc(self.flow, 1)

        getattr(self.blk, eng)(body)
        self.val = v + 1

    def dma_sec(self, eng, fn):
        v = self.val

        def body(e):
            if v > 0:
                e.wait_ge(self.flow, v)
            for ins in fn(e):
                ins.then_inc(self.flow, 16)

        getattr(self.blk, eng)(body)
        self.n_last_dma = None
        raise RuntimeError("use dma_sec_n")

    def dma_sec_n(self, eng, n, fn):
        v, dv, sv = self.val, self.dval, self.sval
        soft = (eng == 'gpsimd')
        sem = self.sflow if soft else self.dflow

        def body(e):
            if v > 0:
                e.wait_ge(self.flow, v)
            if dv > 0:
                e.wait_ge(self.dflow, dv)
            if sv > 0:
                e.wait_ge(self.sflow, sv)
            ins_list = fn(e)
            assert len(ins_list) == n, (len(ins_list), n)
            for ins in ins_list:
                ins.then_inc(sem, 16)

        getattr(self.blk, eng)(body)
        if soft:
            self.sval = sv + 16 * n
        else:
            self.dval = dv + 16 * n

    def fin(self):
        v, dv, sv = self.val, self.dval, self.sval

        def body(e):
            if v > 0:
                e.wait_ge(self.flow, v)
            if dv > 0:
                e.wait_ge(self.dflow, dv)
            if sv > 0:
                e.wait_ge(self.sflow, sv)

        self.blk.sync(body)

    def close(self):
        self.stack.close()
        return self.nc


_SPACED_OPS = {
    'tensor_tensor', 'tensor_add', 'tensor_sub', 'tensor_mul', 'tensor_max',
    'tensor_scalar', 'tensor_single_scalar', 'tensor_scalar_add',
    'tensor_scalar_sub', 'tensor_scalar_mul', 'tensor_scalar_max',
    'tensor_scalar_min', 'tensor_relu', 'tensor_copy', 'tensor_reduce',
    'reduce_max', 'reduce_sum', 'select', 'copy_predicated', 'reciprocal',
    'memset', 'memzero', 'scalar_tensor_tensor', 'tensor_tensor_scan',
    'activation', 'square', 'sqrt', 'copy', 'mul', 'add', 'sign', 'transpose',
}


class _Spaced:
    def __init__(self, e, prog):
        self._e = e
        self._p = prog
        self._pend = None

    def __getattr__(self, name):
        attr = getattr(self._e, name)
        if name not in _SPACED_OPS:
            return attr

        def wrapped(*a, **k):
            if self._pend is not None:
                self._pend.then_inc(self._p._chain, 1)
                self._p._chain_n += 1
                self._e.wait_ge(self._p._chain, self._p._chain_n)
            ins = attr(*a, **k)
            self._pend = ins
            return ins
        return wrapped


def ceil_div(a, b):
    return (a + b - 1) // b


def emit_ln(p, xs, lnout, sq, work, ps_sx, ps_sxx, ps_bc, ones_col, ones_row,
            eps=LN_EPS):
    rb, mb = p._ln_rb, p._ln_mb
    W = lambda s, q=None: work[0:1, s * T + (0 if q is None else q * TT):
                               s * T + (T if q is None else (q + 1) * TT)]

    for q in range(NT):
        qsl = slice(q * TT, (q + 1) * TT)

        def t_sx(te, q=q, qsl=qsl):
            for k in range(NK):
                ins = te.matmul(ps_sx[0:1, :], ones_col[:, 0:1], xs[:, k, qsl],
                                start=(k == 0), stop=(k == NK - 1))
            return ins
        p.sec('tensor', t_sx)
        for kp in range(0, NK, 2):
            p.sec('scalar', lambda s, kp=kp, qsl=qsl: [
                s.square(sq[:, (kp + j) % 2, :], xs[:, kp + j, qsl])
                for j in range(2)])
            p.sec('tensor', lambda te, kp=kp: [
                te.matmul(ps_sxx[0:1, :], ones_col[:, 0:1], sq[:, (kp + j) % 2, :],
                          start=(kp + j == 0), stop=(kp + j == NK - 1))
                for j in range(2)])

        def v_q(v, q=q):
            v.tensor_scalar_mul(W(0, q), ps_sx[0:1, :], 1.0 / D)
            return v.tensor_scalar_mul(W(1, q), ps_sxx[0:1, :], 1.0 / D)
        p.sec('vector', v_q)

    def v_stats(v):
        v.tensor_tensor(W(2), W(0), W(0), ALU.mult)
        v.tensor_tensor(W(1), W(1), W(2), ALU.subtract)
        return v.tensor_scalar_add(W(1), W(1), eps)
    p.sec('vector', v_stats)

    p.sec('scalar', lambda s: s.activation(W(2), W(1), AF.Sqrt))

    def v_fin(v):
        v.reciprocal(W(1), W(2))
        return v.tensor_tensor(W(2), W(0), W(1), ALU.mult)
    p.sec('vector', v_fin)

    for q in range(NT):
        def t_bq(te, q=q):
            te.matmul(ps_bc[:, 0:TT], ones_row[0:1, :], W(1, q),
                      start=True, stop=True)
            return te.matmul(ps_bc[:, TT:2 * TT], ones_row[0:1, :], W(2, q),
                             start=True, stop=True)
        p.sec('tensor', t_bq)

        def v_bq(v, q=q):
            v.tensor_copy(rb[:, q * TT:(q + 1) * TT], ps_bc[:, 0:TT])
            return v.tensor_copy(mb[:, q * TT:(q + 1) * TT], ps_bc[:, TT:2 * TT])
        p.sec('vector', v_bq)

    for k in range(NK):
        def v_apply(v, k=k):
            out = []
            for q in range(NT):
                qsl = slice(q * TT, (q + 1) * TT)
                v.tensor_tensor(sq[:, 0, :], xs[:, k, qsl], rb[:, qsl], ALU.mult)
                out.append(v.tensor_tensor(lnout[:, k, qsl], sq[:, 0, :],
                                           mb[:, qsl], ALU.subtract))
            return out
        p.sec('vector', v_apply)


def alloc_ln(p):
    p._ln_rb = p.sbuf("ln_rb", [128, T], F32)
    p._ln_mb = p.sbuf("ln_mb", [128, T], F32)


def emit_softmax_gates(p, logits_fm, gates_fm, ptm, ps8, identf, negt, mode):
    E = 8
    for i in range(16):
        p.sec('tensor', lambda te, i=i: te.transpose(
            ps8[:, 0:E], logits_fm[0:E, i * 128:(i + 1) * 128], identf[0:E, 0:E]))
        p.sec('vector', lambda v, i=i: v.tensor_copy(ptm[:, i, 0:E], ps8[:, 0:E]))

    def v_sm1(v):
        out = []
        for i in range(16):
            v.reduce_max(ptm[:, i, E:E + 1], ptm[:, i, 0:E], AX.X)
            out.append(v.tensor_scalar_mul(ptm[:, i, E + 1:E + 2],
                                           ptm[:, i, E:E + 1], -1.0))
        return out
    p.sec('vector', v_sm1)

    p.sec('scalar', lambda s: [s.activation(ptm[:, i, 0:E], ptm[:, i, 0:E], AF.Exp,
                                            bias=ptm[:, i, E + 1:E + 2])
                               for i in range(16)])

    def v_sm2(v):
        out = []
        for i in range(16):
            v.reduce_sum(ptm[:, i, E:E + 1], ptm[:, i, 0:E], AX.X)
            v.reciprocal(ptm[:, i, E + 1:E + 2], ptm[:, i, E:E + 1])
            out.append(v.tensor_scalar_mul(ptm[:, i, 0:E], ptm[:, i, 0:E],
                                           ptm[:, i, E + 1:E + 2]))
        return out
    p.sec('vector', v_sm2)

    if mode == 'top2':
        def v_top2(v):
            out = []
            for i in range(16):
                tile = ptm[:, i, 0:E]
                m1 = ptm[:, i, E:E + 1]
                m2 = ptm[:, i, E + 1:E + 2]
                msk = ptm[:, i, E + 2:E + 2 + E]
                tmp = ptm[:, i, E + 2 + E:E + 2 + 2 * E]
                v.reduce_max(m1, tile, AX.X)
                v.tensor_scalar(msk, tile, m1, None, ALU.is_ge)
                v.tensor_scalar_add(tmp, tile, 1e30)
                v.tensor_tensor(tmp, msk, tmp, ALU.mult)
                v.tensor_tensor(tmp, tile, tmp, ALU.subtract)
                v.reduce_max(m2, tmp, AX.X)
                v.tensor_scalar(msk, tile, m2, None, ALU.is_ge)
                out.append(v.tensor_tensor(tile, tile, msk, ALU.mult))
            return out
        p.sec('vector', v_top2)

    for i in range(16):
        p.sec('tensor', lambda te, i=i: te.transpose(
            ps8[0:E, 0:128], ptm[:, i, 0:E], identf[:, :]))
        p.sec('vector', lambda v, i=i: v.tensor_copy(
            gates_fm[0:E, i * 128:(i + 1) * 128], ps8[0:E, 0:128]))


def emit_logits(p, lnbf, wT_sb, ncols, ps, out_fm, bias_sb=None):
    for q in range(NT):
        def t_q(te, q=q):
            for k in range(NK):
                ins = te.matmul(ps[0:ncols, 0:TT], wT_sb[:, k, 0:ncols],
                                lnbf[:, k, q * TT:(q + 1) * TT],
                                start=(k == 0), stop=(k == NK - 1))
            return ins
        p.sec('tensor', t_q)
        if bias_sb is not None:
            p.sec('scalar', lambda s, q=q: s.activation(
                out_fm[0:ncols, q * TT:(q + 1) * TT], ps[0:ncols, 0:TT],
                AF.Identity, bias=bias_sb[0:ncols, 0:1]))
        else:
            p.sec('vector', lambda v, q=q: v.tensor_copy(
                out_fm[0:ncols, q * TT:(q + 1) * TT], ps[0:ncols, 0:TT]))


def build_stage(n_partials, heads):
    p = Prog("stage").start()
    nc = p.nc
    xT = nc.dram_tensor("xT", [D, T], F32, kind="ExternalInput").ap()
    if n_partials:
        pp = nc.dram_tensor("pparts", [n_partials, D, T], F32,
                            kind="ExternalInput").ap()
    xs = p.sbuf("xs", [128, NK, T], F32)
    sq = p.sbuf("sq", [128, 3, TT], F32)
    lnbf = p.sbuf("lnbf", [128, NK, T], BF16)
    work = p.sbuf("work", [1, 3 * T], F32)
    ones_col = p.sbuf("ones_col", [128, 1], F32)
    ones_row = p.sbuf("ones_row", [1, 128], F32)
    alloc_ln(p)
    ps_sx = p.psum("ps_sx", [1, TT], F32)
    ps_sxx = p.psum("ps_sxx", [1, TT], F32)
    ps_bc = p.psum("ps_bc", [128, 2 * TT], F32)
    ps_a = p.psum("ps_a", [128, TT], F32)
    ps_b = ps_a

    outs = {}

    p.sec('vector', lambda v: [v.memset(ones_col[:, :], 1.0),
                               v.memset(ones_row[:, :], 1.0)])

    p.dma_sec_n('sync', NK, lambda e: [
        e.dma_start(out=xs[:, k, :], in_=xT[k * 128:(k + 1) * 128, :])
        for k in range(NK)])

    if n_partials:
        ptmp = p.sbuf("ptmp", [128, T], F32)
        for k in range(NK):
            for j in range(n_partials):
                p.dma_sec_n('sync', 1, lambda e, k=k, j=j: [
                    e.dma_start(out=ptmp[:, :], in_=pp[j, k * 128:(k + 1) * 128, :])])
                p.sec('vector', lambda v, k=k: v.scalar_tensor_tensor(
                    xs[:, k, :], ptmp[:, :], 0.5, xs[:, k, :], ALU.mult, ALU.add))

    if 'out_x' in heads:
        xo = nc.dram_tensor("x_out", [D, T], F32, kind="ExternalOutput").ap()
        p.dma_sec_n('sync', NK, lambda e: [
            e.dma_start(out=xo[k * 128:(k + 1) * 128, :], in_=xs[:, k, :])
            for k in range(NK)])

    emit_ln(p, xs, lnbf, sq, work, ps_sx, ps_sxx, ps_bc, ones_col, ones_row)

    if 'gates0' in heads or 'probs1' in heads:
        mode = 'top2' if 'gates0' in heads else 'probs'
        gw = nc.dram_tensor("gwT", [D, 8], F32, kind="ExternalInput").ap()
        idf = nc.dram_tensor("identf", [128, 128], F32, kind="ExternalInput").ap()
        gates_out = nc.dram_tensor("gates", [8, T], F32, kind="ExternalOutput").ap()
        lnout_hbm = nc.dram_tensor("lnbf_out", [D, T], BF16, kind="ExternalOutput").ap()
        gw_sb = p.sbuf("gw_sb", [128, NK, 8], BF16)
        idf_sb = p.sbuf("idf_sb", [128, 128], F32)
        gfm = p.sbuf("gfm", [8, T], F32)
        ptm = p.sbuf("ptm", [128, 16, 26], F32)
        negt = p.sbuf("negt", [128, 8], F32)
        gb_sb = p.sbuf("gb_sb", [8, 1], F32)
        if mode == 'top2':
            gb = nc.dram_tensor("gb", [8], F32, kind="ExternalInput").ap()
            p.dma_sec_n('sync', 2, lambda e: [
                e.dma_start(out=idf_sb[:, :], in_=idf[:, :]),
                e.dma_start(out=gb_sb[:, 0], in_=gb[:])])
        else:
            p.dma_sec_n('sync', 1, lambda e: [
                e.dma_start(out=idf_sb[:, :], in_=idf[:, :])])
        p.dma_sec_n('gpsimd', 1, lambda e: [
            e.dma_start(out=gw_sb[:, :, :],
                        in_=gw.rearrange("(k p) e -> p k e", p=128))])
        p.sec('vector', lambda v: v.memset(negt[:, :], -1e30))
        emit_logits(p, lnbf, gw_sb, 8, ps_a, gfm,
                    bias_sb=gb_sb if mode == 'top2' else None)
        emit_softmax_gates(p, gfm, gfm, ptm, ps_bc, idf_sb, negt, mode)
        p.dma_sec_n('sync', 1 + NK, lambda e: [
            e.dma_start(out=gates_out[:, :], in_=gfm[:, :])] + [
            e.dma_start(out=lnout_hbm[k * 128:(k + 1) * 128, :], in_=lnbf[:, k, :])
            for k in range(NK)])

    if 'projL0' in heads:
        emit_projL0(p, nc, lnbf, ps_a, ps_bc, ones_row)

    if 'qkv1' in heads:
        emit_qkv1(p, nc, lnbf, xs, sq, work, ps_a, ps_b)

    p.fin()
    return p.close()


def emit_projL0(p, nc, lnbf, ps_a, ps_bc, ones_row):
    wq = nc.dram_tensor("WqT", [D, D], F32, kind="ExternalInput").ap()
    wk = nc.dram_tensor("WkT", [D, D], F32, kind="ExternalInput").ap()
    wv = nc.dram_tensor("WvT", [D, D], F32, kind="ExternalInput").ap()
    wb = nc.dram_tensor("WbetaT", [D, NHEAD], F32, kind="ExternalInput").ap()
    bd2 = nc.dram_tensor("blockdiag2", [128, 2], F32, kind="ExternalInput").ap()
    bc64 = nc.dram_tensor("bcast64", [2, 128], F32, kind="ExternalInput").ap()
    idb = nc.dram_tensor("identf", [128, 128], F32, kind="ExternalInput").ap()
    q_lm = nc.dram_tensor("q_lm", [NHEAD, T, HD], F32, kind="ExternalOutput").ap()
    k_lm = nc.dram_tensor("k_lm", [NHEAD, T, HD], F32, kind="ExternalOutput").ap()
    v_lm = nc.dram_tensor("v_lm", [NHEAD, T, HD], F32, kind="ExternalOutput").ap()
    beta_o = nc.dram_tensor("beta", [NHEAD, T], F32, kind="ExternalOutput").ap()

    w_sb = p.sbuf("w_sb", [128, NK, D], BF16)
    pt = p.sbuf("pt", [128, TT], F32)
    sqt = p.sbuf("sqt", [128, TT], F32)
    hsum = p.sbuf("hsum", [2, TT], F32)
    obf = p.sbuf("obf", [128, NK, TT], F32)
    tmt = p.sbuf("tmt", [128, NHEAD * HD], F32)
    bd2_sb = p.sbuf("bd2_sb", [128, 2], F32)
    bc64_sb = p.sbuf("bc64_sb", [2, 128], F32)
    idb_sb = p.sbuf("idb_sb", [128, 128], F32)
    bsb = p.sbuf("bsb", [NHEAD, TT], F32)
    wbsb = p.sbuf("wbsb", [128, NK, NHEAD], BF16)
    ps_t = p.psum("ps_t", [128, HD], F32)

    p.dma_sec_n('sync', 3, lambda e: [
        e.dma_start(out=bd2_sb[:, :], in_=bd2[:, :]),
        e.dma_start(out=bc64_sb[:, :], in_=bc64[:, :]),
        e.dma_start(out=idb_sb[:, :], in_=idb[:, :])])

    def proj_one(wap, out_hbm, do_l2):
        p.dma_sec_n('gpsimd', 1, lambda e: [
            e.dma_start(out=w_sb[:, :, :],
                        in_=wap.rearrange("(k p) o -> p k o", p=128))])
        for tt_i in range(NT):
            tsl = slice(tt_i * TT, (tt_i + 1) * TT)
            for oc in range(NK):
                def t_mm(te, oc=oc, tsl=tsl):
                    for k in range(NK):
                        ins = te.matmul(ps_a[:, :], w_sb[:, k, oc * 128:(oc + 1) * 128],
                                        lnbf[:, k, tsl], start=(k == 0),
                                        stop=(k == NK - 1))
                    return ins
                p.sec('tensor', t_mm)
                if do_l2:
                    p.sec('scalar', lambda s: s.activation(pt[:, :], ps_a[:, :], AF.Silu))
                    p.sec('scalar', lambda s: s.square(sqt[:, :], pt[:, :]))
                    p.sec('tensor', lambda te: te.matmul(
                        ps_bc[0:2, 0:TT], bd2_sb[:, :], sqt[:, :],
                        start=True, stop=True))
                    p.sec('vector', lambda v: v.tensor_copy(hsum[:, :], ps_bc[0:2, 0:TT]))
                    p.sec('scalar', lambda s: s.activation(hsum[:, :], hsum[:, :], AF.Sqrt))
                    p.sec('vector', lambda v: [
                        v.tensor_scalar_add(hsum[:, :], hsum[:, :], 1e-6),
                        v.reciprocal(hsum[:, :], hsum[:, :])])
                    p.sec('tensor', lambda te: te.matmul(
                        ps_bc[:, 0:TT], bc64_sb[:, :], hsum[:, :],
                        start=True, stop=True))
                    p.sec('vector', lambda v, oc=oc: v.tensor_tensor(
                        obf[:, oc, :], pt[:, :], ps_bc[:, 0:TT], ALU.mult))
                else:
                    p.sec('scalar', lambda s, oc=oc: s.activation(
                        obf[:, oc, :], ps_a[:, :], AF.Silu))
            for c4 in range(4):
                c = tt_i * 4 + c4
                for lane in range(NHEAD):
                    oc, half = lane // 2, (lane % 2) * 64
                    p.sec('tensor', lambda te, oc=oc, half=half, c4=c4: te.transpose(
                        ps_t[:, :], obf[half:half + 64, oc, c4 * 128:(c4 + 1) * 128],
                        idb_sb[half:half + 64, half:half + 64]))
                    p.sec('vector', lambda v, lane=lane: v.tensor_copy(
                        tmt[:, lane * HD:(lane + 1) * HD], ps_t[:, :]))
                p.dma_sec_n('sync', 1, lambda e, c=c: [
                    e.dma_start(out=out_hbm[:, c * 128:(c + 1) * 128, :]
                                .rearrange("l p d -> p l d"),
                                in_=tmt.rearrange("p (l d) -> p l d", l=NHEAD))])

    proj_one(wq, q_lm, True)
    proj_one(wk, k_lm, True)
    proj_one(wv, v_lm, False)

    p.dma_sec_n('gpsimd', 1, lambda e: [
        e.dma_start(out=wbsb[:, :, :], in_=wb.rearrange("(k p) e -> p k e", p=128))])
    for q in range(NT):
        def t_b(te, q=q):
            for k in range(NK):
                ins = te.matmul(ps_a[0:NHEAD, :], wbsb[:, k, :],
                                lnbf[:, k, q * TT:(q + 1) * TT],
                                start=(k == 0), stop=(k == NK - 1))
            return ins
        p.sec('tensor', t_b)
        p.sec('scalar', lambda s: s.activation(bsb[:, :], ps_a[0:NHEAD, :], AF.Sigmoid))
        p.dma_sec_n('sync', 1, lambda e, q=q: [
            e.dma_start(out=beta_o[:, q * TT:(q + 1) * TT], in_=bsb[:, :])])


def emit_qkv1(p, nc, lnbf, xs, sq, work, ps_a, ps_b):
    wqkv = nc.dram_tensor("wqkvP", [D, 3 * D], F32, kind="ExternalInput").ap()
    xln_o = nc.dram_tensor("xln_out", [D, T], F32, kind="ExternalOutput").ap()
    outs = [nc.dram_tensor(n, [D, T], BF16, kind="ExternalOutput").ap()
            for n in ("q4", "k4", "v4")]
    w_sb = p.sbuf("wqkv_sb", [128, NK, D], BF16)
    ofm = p.sbuf("ofm", [128, NK, TT], BF16)
    xlt = p.sbuf("xlt", [128, T], F32)

    for k in range(NK):
        def v_xln(v, k=k):
            v.tensor_tensor(xlt[:, :], xs[:, k, :], p._ln_rb[:, :], ALU.mult)
            return v.tensor_tensor(xlt[:, :], xlt[:, :], p._ln_mb[:, :],
                                   ALU.subtract)
        p.sec('vector', v_xln)
        p.dma_sec_n('sync', 1, lambda e, k=k: [
            e.dma_start(out=xln_o[k * 128:(k + 1) * 128, :], in_=xlt[:, :])])

    for oi, oap in enumerate(outs):
        p.dma_sec_n('gpsimd', 1, lambda e, oi=oi: [
            e.dma_start(out=w_sb[:, :, :],
                        in_=wqkv[:, oi * D:(oi + 1) * D]
                        .rearrange("(k p) o -> p k o", p=128))])
        for tt_i in range(NT):
            for oc in range(NK):
                psx = ps_a if oc % 2 == 0 else ps_b

                def t_mm(te, oc=oc, tt_i=tt_i, psx=psx):
                    for k in range(NK):
                        ins = te.matmul(psx[:, :], w_sb[:, k, oc * 128:(oc + 1) * 128],
                                        lnbf[:, k, tt_i * TT:(tt_i + 1) * TT],
                                        start=(k == 0), stop=(k == NK - 1))
                    return ins
                p.sec('tensor', t_mm)
                p.sec('vector', lambda v, oc=oc, psx=psx: v.tensor_copy(
                    ofm[:, oc, :], psx[:, :]))
            p.dma_sec_n('sync', 1, lambda e, tt_i=tt_i: [
                e.dma_start(out=oap[:, tt_i * TT:(tt_i + 1) * TT]
                            .rearrange("(k p) c -> p k c", p=128),
                            in_=ofm[:, :, :])])


def build_experts(kind, ne):
    H = 4096 if kind == 'L0' else 2048
    NH = H // 128
    act = AF.Relu if kind == 'L0' else AF.Gelu
    p = Prog("exp").start()
    nc = p.nc
    lnin = nc.dram_tensor("lnbf_in", [D, T], BF16, kind="ExternalInput").ap()
    gts = nc.dram_tensor("gates", [8, T], F32, kind="ExternalInput").ap()
    sel = nc.dram_tensor("sel2", [8, ne * 128], F32, kind="ExternalInput").ap()
    w1 = nc.dram_tensor("w1", [ne, D, H], F32, kind="ExternalInput").ap()
    w2 = nc.dram_tensor("w2", [ne, H, D], F32, kind="ExternalInput").ap()
    if kind == 'L0':
        b1 = nc.dram_tensor("b1", [ne, 128, NH], F32, kind="ExternalInput").ap()
        b2 = nc.dram_tensor("b2", [ne, 128, 8], F32, kind="ExternalInput").ap()
    parts = nc.dram_tensor("partials", [ne, D, T], F32, kind="ExternalOutput").ap()

    w1s = p.sbuf("w1s", [128, NK, H], BF16)
    w2s = p.sbuf("w2s", [128, NH, D], BF16)
    lnt = p.sbuf("lnt", [128, NK, TT], BF16)
    hs = p.sbuf("hs", [128, NH, TT], BF16)
    ytmp = p.sbuf("ytmp", [128, 2 * TT], F32)
    ybuf = p.sbuf("ybuf", [128, NK, TT], F32)
    gateb = p.sbuf("gateb", [128, TT], F32)
    gts_sb = p.sbuf("gts_sb", [8, T], F32)
    sel_sb = p.sbuf("sel_sb", [8, ne * 128], F32)
    if kind == 'L0':
        b1s = p.sbuf("b1s", [128, NH], F32)
        b2s = p.sbuf("b2s", [128, 8], F32)
    psA = p.psum("psA", [128, TT], F32)
    psB = p.psum("psB", [128, TT], F32)
    psG = p.psum("psG", [128, TT], F32)

    p.dma_sec_n('sync', 2, lambda e: [
        e.dma_start(out=gts_sb[:, :], in_=gts[:, :]),
        e.dma_start(out=sel_sb[:, :], in_=sel[:, :])])

    for i in range(ne):
        nw = 1 + 4 + (2 if kind == 'L0' else 0)

        def gw(e, i=i):
            out = [e.dma_start(out=w1s[:, :, :],
                               in_=w1[i, :, :].rearrange("(k p) h -> p k h", p=128))]
            for j in range(4):
                lo = NH // 4 * j
                hi = NH // 4 * (j + 1)
                out.append(e.dma_start(
                    out=w2s[:, lo:hi, :],
                    in_=w2[i, lo * 128:hi * 128, :].rearrange("(k p) o -> p k o", p=128)))
            if kind == 'L0':
                out.append(e.dma_start(out=b1s[:, :], in_=b1[i, :, :]))
                out.append(e.dma_start(out=b2s[:, :], in_=b2[i, :, :]))
            return out
        p.dma_sec_n('gpsimd', nw, gw)

        for tt_i in range(NT):
            tsl = slice(tt_i * TT, (tt_i + 1) * TT)
            p.dma_sec_n('sync', 1, lambda e, tsl=tsl: [
                e.dma_start(out=lnt[:, :, :],
                            in_=lnin[:, tsl].rearrange("(k p) c -> p k c", p=128))])
            p.sec('tensor', lambda te, i=i, tsl=tsl: te.matmul(
                psG[:, :], sel_sb[:, i * 128:(i + 1) * 128], gts_sb[:, tsl],
                start=True, stop=True))
            p.sec('vector', lambda v: v.tensor_copy(gateb[:, :], psG[:, :]))
            for hc0 in range(0, NH, 2):
                def t_h(te, hc0=hc0):
                    for j, psx in ((0, psA), (1, psB)):
                        hc = hc0 + j
                        for k in range(NK):
                            ins = te.matmul(psx[:, :],
                                            w1s[:, k, hc * 128:(hc + 1) * 128],
                                            lnt[:, k, :], start=(k == 0),
                                            stop=(k == NK - 1))
                    return ins
                p.sec('tensor', t_h)

                def s_h(s, hc0=hc0):
                    out = []
                    for j, psx in ((0, psA), (1, psB)):
                        hc = hc0 + j
                        if kind == 'L0':
                            out.append(s.activation(hs[:, hc, :], psx[:, :], act,
                                                    bias=b1s[:, hc:hc + 1]))
                        else:
                            out.append(s.activation(hs[:, hc, :], psx[:, :], act))
                    return out
                p.sec('scalar', s_h)
            for oc0 in range(0, NK, 2):
                def t_o(te, oc0=oc0):
                    for j, psx in ((0, psA), (1, psB)):
                        oc = oc0 + j
                        for k in range(NH):
                            ins = te.matmul(psx[:, :],
                                            w2s[:, k, oc * 128:(oc + 1) * 128],
                                            hs[:, k, :], start=(k == 0),
                                            stop=(k == NH - 1))
                    return ins
                p.sec('tensor', t_o)

                def s_o(s, oc0=oc0):
                    out = []
                    for j, psx in ((0, psA), (1, psB)):
                        oc = oc0 + j
                        if kind == 'L0':
                            out.append(s.activation(ytmp[:, j * TT:(j + 1) * TT],
                                                    psx[:, :], AF.Identity,
                                                    bias=b2s[:, oc:oc + 1]))
                        else:
                            out.append(s.copy(ytmp[:, j * TT:(j + 1) * TT],
                                              psx[:, :]))
                    return out
                p.sec('scalar', s_o)

                def v_o(v, oc0=oc0):
                    return [v.tensor_tensor(ybuf[:, oc0 + j, :],
                                            ytmp[:, j * TT:(j + 1) * TT],
                                            gateb[:, :], ALU.mult)
                            for j in range(2)]
                p.sec('vector', v_o)
            p.dma_sec_n('sync', 1, lambda e, i=i, tsl=tsl: [
                e.dma_start(out=parts[i, :, tsl].rearrange("(k p) c -> p k c", p=128),
                            in_=ybuf[:, :, :])])
    p.fin()
    return p.close()


def build_scan():
    p = Prog("scan").start()
    nc = p.nc
    qp = nc.dram_tensor("qp", [NHEAD, 512, HD], F32, kind="ExternalInput").ap()
    kp = nc.dram_tensor("kp", [NHEAD, 512, HD], F32, kind="ExternalInput").ap()
    vp = nc.dram_tensor("vp", [NHEAD, 512, HD], F32, kind="ExternalInput").ap()
    bp = nc.dram_tensor("betap", [NHEAD, 512], F32, kind="ExternalInput").ap()
    mi = nc.dram_tensor("M_in", [NHEAD, HD * HD], F32, kind="ExternalInput").ap()
    mo = nc.dram_tensor("M_out", [NHEAD, HD * HD], F32, kind="ExternalOutput").ap()
    op = nc.dram_tensor("o_part", [512, D], F32, kind="ExternalOutput").ap()

    M = p.sbuf("M", [NHEAD, HD, HD], F32)
    qc = p.sbuf("qc", [NHEAD, 128, HD], F32)
    kc = p.sbuf("kc", [NHEAD, 128, HD], F32)
    vc = p.sbuf("vc", [NHEAD, 128, HD], F32)
    bsb = p.sbuf("bsb", [NHEAD, 512], F32)
    osb = p.sbuf("osb", [NHEAD, 128, HD], F32)
    s1 = p.sbuf("s1", [NHEAD, HD, HD], F32)
    s2 = p.sbuf("s2", [NHEAD, HD, HD], F32)
    dl = p.sbuf("dl", [NHEAD, 2, HD], F32)

    p.dma_sec_n('sync', 2, lambda e: [
        e.dma_start(out=M.rearrange("p a b -> p (a b)")[:, :], in_=mi[:, :]),
        e.dma_start(out=bsb[:, :], in_=bp[:, :])])

    Mba = M.rearrange("p a b -> p b a")

    for c in range(4):
        csl = slice(c * 128, (c + 1) * 128)
        p.dma_sec_n('sync', 3, lambda e, csl=csl: [
            e.dma_start(out=qc[:, :, :], in_=qp[:, csl, :]),
            e.dma_start(out=kc[:, :, :], in_=kp[:, csl, :]),
            e.dma_start(out=vc[:, :, :], in_=vp[:, csl, :])])

        def steps(v, c=c):
            for tl in range(128):
                t = c * 128 + tl
                q_t = qc[:, tl, :]
                k_t = kc[:, tl, :]
                v_t = vc[:, tl, :]
                v.tensor_tensor(s1[:, :, :], Mba,
                                q_t.unsqueeze(1).broadcast_to([NHEAD, HD, HD]),
                                ALU.mult)
                v.tensor_reduce(osb[:, tl, :], s1[:, :, :], AX.X, ALU.add)
                v.tensor_tensor(s2[:, :, :], M[:, :, :],
                                k_t.unsqueeze(1).broadcast_to([NHEAD, HD, HD]),
                                ALU.mult)
                v.tensor_reduce(dl[:, 0, :], s2[:, :, :], AX.X, ALU.add)
                v.tensor_tensor(dl[:, 1, :], v_t, dl[:, 0, :], ALU.subtract)
                v.scalar_tensor_tensor(
                    s2[:, :, :],
                    k_t.unsqueeze(2).broadcast_to([NHEAD, HD, HD]),
                    bsb[:, t:t + 1],
                    dl[:, 1, :].unsqueeze(1).broadcast_to([NHEAD, HD, HD]),
                    ALU.mult, ALU.mult)
                ins = v.tensor_tensor(M[:, :, :], M[:, :, :], s2[:, :, :], ALU.add)
            return ins
        p.sec('vector', steps)

        p.dma_sec_n('sync', 1, lambda e, csl=csl: [
            e.dma_start(out=op[csl, :].rearrange("t (l d) -> l t d", l=NHEAD),
                        in_=osb[:, :, :])])

    p.dma_sec_n('sync', 1, lambda e: [
        e.dma_start(out=mo[:, :], in_=M.rearrange("p a b -> p (a b)")[:, :])])
    p.fin()
    return p.close()


def build_rmswo():
    p = Prog("rmswo").start()
    nc = p.nc
    of = nc.dram_tensor("o_full", [T, D], F32, kind="ExternalInput").ap()
    x1 = nc.dram_tensor("xT", [D, T], F32, kind="ExternalInput").ap()
    wo = nc.dram_tensor("WoT", [D, D], F32, kind="ExternalInput").ap()
    rw = nc.dram_tensor("rmsw_rep", [128, D], F32, kind="ExternalInput").ap()
    idb = nc.dram_tensor("identb", [128, 128], BF16, kind="ExternalInput").ap()

    xs = p.sbuf("xs", [128, NK, T], F32)
    sq = p.sbuf("sq", [128, 3, TT], F32)
    lnbf = p.sbuf("lnbf", [128, NK, T], BF16)
    onT = p.sbuf("onT", [128, NK, TT], BF16)
    otile = p.sbuf("otile", [128, D], F32)
    rr = p.sbuf("rr", [128, 4], F32)
    onbf = p.sbuf("onbf", [128, D], BF16)
    rw_sb = p.sbuf("rw_sb", [128, D], F32)
    idb_sb = p.sbuf("idb_sb", [128, 128], BF16)
    wo_sb = p.sbuf("wo_sb", [128, NK, D], BF16)
    work = p.sbuf("work", [1, 3 * T], F32)
    ones_col = p.sbuf("ones_col", [128, 1], F32)
    ones_row = p.sbuf("ones_row", [1, 128], F32)
    alloc_ln(p)
    ps_sx = p.psum("ps_sx", [1, TT], F32)
    ps_sxx = p.psum("ps_sxx", [1, TT], F32)
    ps_bc = p.psum("ps_bc", [128, 2 * TT], F32)
    ps_a = p.psum("ps_a", [128, TT], F32)
    ps_b = ps_a
    ps_t = p.psum("ps_t", [128, 128], BF16)

    p.sec('vector', lambda v: [v.memset(ones_col[:, :], 1.0),
                               v.memset(ones_row[:, :], 1.0)])
    p.dma_sec_n('sync', 2, lambda e: [e.dma_start(out=rw_sb[:, :], in_=rw[:, :]),
                                      e.dma_start(out=idb_sb[:, :], in_=idb[:, :])])
    p.dma_sec_n('gpsimd', 1, lambda e: [
        e.dma_start(out=wo_sb[:, :, :], in_=wo.rearrange("(k p) o -> p k o", p=128))])

    p.dma_sec_n('sync', NK, lambda e: [
        e.dma_start(out=xs[:, k, :], in_=x1[k * 128:(k + 1) * 128, :])
        for k in range(NK)])
    for tt_i in range(NT):
        tsl = slice(tt_i * TT, (tt_i + 1) * TT)
        for c4 in range(4):
            ci = tt_i * 4 + c4
            p.dma_sec_n('sync', 1, lambda e, ci=ci: [
                e.dma_start(out=otile[:, :], in_=of[ci * 128:(ci + 1) * 128, :])])
            p.sec('scalar', lambda s: s.square(ps_bc[:, 0:D], otile[:, :]))

            def v_m(v):
                v.reduce_sum(rr[:, 0:1], ps_bc[:, 0:D], AX.X)
                v.tensor_scalar(rr[:, 1:2], rr[:, 0:1], 1.0 / D, 1e-6,
                                ALU.mult, ALU.add)
                return v.tensor_scalar_mul(rr[:, 1:2], rr[:, 1:2], 2.0 ** -64)
            p.sec('vector', v_m)
            p.sec('scalar', lambda s: s.activation(rr[:, 2:3], rr[:, 1:2], AF.Sqrt))

            def v_n(v):
                v.reciprocal(rr[:, 0:1], rr[:, 2:3])
                v.tensor_scalar_mul(rr[:, 0:1], rr[:, 0:1], 2.0 ** -32)
                v.tensor_scalar_mul(otile[:, :], otile[:, :], rr[:, 0:1])
                return v.tensor_tensor(onbf[:, :], otile[:, :], rw_sb[:, :], ALU.mult)
            p.sec('vector', v_n)
            for k in range(NK):
                p.sec('tensor', lambda te, k=k: te.transpose(
                    ps_t[:, :], onbf[:, k * 128:(k + 1) * 128], idb_sb[:, :]))
                p.sec('vector', lambda v, k=k, c4=c4: v.tensor_copy(
                    onT[:, k, c4 * 128:(c4 + 1) * 128], ps_t[:, :]))
        for oc in range(NK):
            def t_mm(te, oc=oc):
                for k in range(NK):
                    ins = te.matmul(ps_a[:, :], wo_sb[:, k, oc * 128:(oc + 1) * 128],
                                    onT[:, k, :], start=(k == 0), stop=(k == NK - 1))
                return ins
            p.sec('tensor', t_mm)
            p.sec('vector', lambda v, oc=oc, tsl=tsl: v.tensor_tensor(
                xs[:, oc, tsl], xs[:, oc, tsl], ps_a[:, :], ALU.add))
    xo = nc.dram_tensor("x_out", [D, T], F32, kind="ExternalOutput").ap()
    p.dma_sec_n('sync', NK, lambda e: [
        e.dma_start(out=xo[k * 128:(k + 1) * 128, :], in_=xs[:, k, :])
        for k in range(NK)])

    emit_ln(p, xs, lnbf, sq, work, ps_sx, ps_sxx, ps_bc, ones_col, ones_row)
    gw = nc.dram_tensor("gwT", [D, 8], F32, kind="ExternalInput").ap()
    gb = nc.dram_tensor("gb", [8], F32, kind="ExternalInput").ap()
    idf = nc.dram_tensor("identf", [128, 128], F32, kind="ExternalInput").ap()
    gates_out = nc.dram_tensor("gates", [8, T], F32, kind="ExternalOutput").ap()
    lnout_hbm = nc.dram_tensor("lnbf_out", [D, T], BF16, kind="ExternalOutput").ap()
    gw_sb = p.sbuf("gw_sb", [128, NK, 8], BF16)
    idf_sb = p.sbuf("idf_sb", [128, 128], F32)
    gfm = p.sbuf("gfm", [8, T], F32)
    ptm = p.sbuf("ptm", [128, 16, 26], F32)
    negt = p.sbuf("negt", [128, 8], F32)
    gb_sb = p.sbuf("gb_sb", [8, 1], F32)
    p.dma_sec_n('sync', 2, lambda e: [e.dma_start(out=idf_sb[:, :], in_=idf[:, :]),
                                      e.dma_start(out=gb_sb[:, 0], in_=gb[:])])
    p.dma_sec_n('gpsimd', 1, lambda e: [
        e.dma_start(out=gw_sb[:, :, :], in_=gw.rearrange("(k p) e -> p k e", p=128))])
    p.sec('vector', lambda v: v.memset(negt[:, :], -1e30))
    emit_logits(p, lnbf, gw_sb, 8, ps_a, gfm, bias_sb=gb_sb)
    emit_softmax_gates(p, gfm, gfm, ptm, ps_bc, idf_sb, negt, 'top2')
    p.dma_sec_n('sync', 1 + NK, lambda e: [
        e.dma_start(out=gates_out[:, :], in_=gfm[:, :])] + [
        e.dma_start(out=lnout_hbm[k * 128:(k + 1) * 128, :], in_=lnbf[:, k, :])
        for k in range(NK)])
    p.fin()
    return p.close()


def build_route():
    p = Prog("route").start()
    nc = p.nc
    pr = nc.dram_tensor("p_resh", [128, 1024], F32, kind="ExternalInput").ap()
    po = nc.dram_tensor("probs_own", [8, T], F32, kind="ExternalInput").ap()
    go = nc.dram_tensor("groupones", [128, 8], F32, kind="ExternalInput").ap()
    sg = nc.dram_tensor("selg", [8, 128], F32, kind="ExternalInput").ap()
    pk = nc.dram_tensor("pick", [128, 8], F32, kind="ExternalInput").ap()
    idf = nc.dram_tensor("identf", [128, 128], F32, kind="ExternalInput").ap()
    gout = nc.dram_tensor("gates", [8, T], F32, kind="ExternalOutput").ap()

    prs = p.sbuf("prs", [128, 1024], F32)
    pos = p.sbuf("pos", [8, T], F32)
    gos = p.sbuf("gos", [128, 8], F32)
    sgs = p.sbuf("sgs", [8, 128], F32)
    pks = p.sbuf("pks", [128, 8], F32)
    idfs = p.sbuf("idfs", [128, 128], F32)
    lo = p.sbuf("lo", [128, 1], F32)
    hi = p.sbuf("hi", [128, 1], F32)
    mid = p.sbuf("mid", [128, 3], F32)
    cmp_t = p.sbuf("cmp_t", [128, 1024], F32)
    c8 = p.sbuf("c8", [8, 2], F32)
    th = p.sbuf("th", [128, 8], F32)
    ptm = p.sbuf("ptm", [128, 16, 26], F32)
    negt = p.sbuf("negt", [128, 8], F32)
    gfm = p.sbuf("gfm", [8, T], F32)
    ps_r = p.psum("ps_r", [128, 128], F32)

    p.dma_sec_n('sync', 6, lambda e: [
        e.dma_start(out=prs[:, :], in_=pr[:, :]),
        e.dma_start(out=pos[:, :], in_=po[:, :]),
        e.dma_start(out=gos[:, :], in_=go[:, :]),
        e.dma_start(out=sgs[:, :], in_=sg[:, :]),
        e.dma_start(out=pks[:, :], in_=pk[:, :]),
        e.dma_start(out=idfs[:, :], in_=idf[:, :])])
    p.sec('vector', lambda v: [v.memset(lo[:, :], 0.0), v.memset(hi[:, :], 1.0),
                               v.memset(negt[:, :], -1e30)])

    CAP = float(T * 8)
    for it in range(34):
        def v_mid(v):
            v.tensor_tensor(mid[:, 0:1], lo[:, :], hi[:, :], ALU.add)
            v.tensor_scalar_mul(mid[:, 0:1], mid[:, 0:1], 0.5)
            v.tensor_scalar(cmp_t[:, :], prs[:, :], mid[:, 0:1], None, ALU.is_gt)
            return v.reduce_sum(mid[:, 1:2], cmp_t[:, :], AX.X)
        p.sec('vector', v_mid)
        p.sec('tensor', lambda te: te.matmul(ps_r[0:8, 0:1], gos[:, :], mid[:, 1:2],
                                             start=True, stop=True))
        p.sec('vector', lambda v: v.tensor_copy(c8[:, 0:1], ps_r[0:8, 0:1]))
        p.sec('tensor', lambda te: te.matmul(ps_r[:, 0:1], sgs[:, :], c8[:, 0:1],
                                             start=True, stop=True))

        def v_upd(v):
            v.tensor_copy(mid[:, 1:2], ps_r[:, 0:1])
            v.tensor_scalar(mid[:, 2:3], mid[:, 1:2], 2048.0, None, ALU.is_ge)
            v.tensor_tensor(mid[:, 1:2], mid[:, 0:1], lo[:, :], ALU.subtract)
            v.tensor_tensor(mid[:, 1:2], mid[:, 2:3], mid[:, 1:2], ALU.mult)
            v.tensor_tensor(lo[:, :], lo[:, :], mid[:, 1:2], ALU.add)
            v.tensor_tensor(mid[:, 1:2], mid[:, 0:1], hi[:, :], ALU.subtract)
            v.tensor_tensor(hi[:, :], hi[:, :], mid[:, 1:2], ALU.add)
            v.tensor_tensor(mid[:, 1:2], mid[:, 2:3], mid[:, 1:2], ALU.mult)
            return v.tensor_tensor(hi[:, :], hi[:, :], mid[:, 1:2], ALU.subtract)
        p.sec('vector', v_upd)

    p.sec('tensor', lambda te: te.matmul(ps_r[0:1, 0:8], lo[:, :], pks[:, :],
                                         start=True, stop=True))
    p.sec('vector', lambda v: v.tensor_copy(c8[0:1, 0:2].bitcast(F32), ps_r[0:1, 0:2]))
    p.sec('vector', lambda v: v.tensor_copy(th[0:1, 0:8], ps_r[0:1, 0:8]))
    ones_row = p.sbuf("ones_row", [1, 128], F32)
    p.sec('vector', lambda v: v.memset(ones_row[:, :], 1.0))
    p.sec('tensor', lambda te: te.matmul(ps_r[:, 0:8], ones_row[0:1, :], th[0:1, 0:8],
                                         start=True, stop=True))
    p.sec('vector', lambda v: v.tensor_copy(th[:, :], ps_r[:, 0:8]))

    for i in range(16):
        p.sec('tensor', lambda te, i=i: te.transpose(
            ps_r[:, 0:8], pos[0:8, i * 128:(i + 1) * 128], idfs[0:8, 0:8]))
        p.sec('vector', lambda v, i=i: v.tensor_copy(ptm[:, i, 0:8], ps_r[:, 0:8]))

    def v_gate(v):
        out = []
        for i in range(16):
            tile = ptm[:, i, 0:8]
            msk = ptm[:, i, 8:16]
            mskd = ptm[:, i, 16:24]
            a1 = ptm[:, i, 24:25]
            a2 = ptm[:, i, 25:26]
            v.tensor_tensor(msk, tile, th[:, :], ALU.is_gt)
            v.tensor_tensor(mskd, msk, tile, ALU.mult)
            v.tensor_scalar(msk, msk, 1e30, -1e30, ALU.mult, ALU.add)
            v.tensor_tensor(mskd, mskd, msk, ALU.add)
            v.reduce_max(a1, mskd, AX.X)
            v.reduce_max(a2, tile, AX.X)
            v.tensor_scalar(msk[:, 0:1], a1, -1e29, None, ALU.is_gt)
            v.tensor_tensor(mskd[:, 0:1], a1, a2, ALU.subtract)
            v.tensor_tensor(mskd[:, 0:1], msk[:, 0:1], mskd[:, 0:1], ALU.mult)
            v.tensor_tensor(a2, a2, mskd[:, 0:1], ALU.add)
            v.tensor_scalar(msk, tile, a2, None, ALU.is_equal)
            out.append(v.tensor_scalar(tile, msk, a2, None, ALU.mult))
        return out
    p.sec('vector', v_gate)

    for i in range(16):
        p.sec('tensor', lambda te, i=i: te.transpose(
            ps_r[0:8, :], ptm[:, i, 0:8], idfs[:, :]))
        p.sec('vector', lambda v, i=i: v.tensor_copy(
            gfm[0:8, i * 128:(i + 1) * 128], ps_r[0:8, 0:128]))
    p.dma_sec_n('sync', 1, lambda e: [e.dma_start(out=gout[:, :], in_=gfm[:, :])])
    p.fin()
    return p.close()


def build_attn1():
    p = Prog("attn").start()
    nc = p.nc
    q4 = nc.dram_tensor("q4", [D, T], BF16, kind="ExternalInput").ap()
    k4 = nc.dram_tensor("k4", [D, T], BF16, kind="ExternalInput").ap()
    v4 = nc.dram_tensor("v4", [D, T], BF16, kind="ExternalInput").ap()
    idb = nc.dram_tensor("identb", [128, 128], BF16, kind="ExternalInput").ap()
    ao = nc.dram_tensor("attnout", [D, T], BF16, kind="ExternalOutput").ap()

    qs = p.sbuf("qs", [128, NK, T], BF16)
    ks = p.sbuf("ks", [128, NK, T], BF16)
    vs = p.sbuf("vs", [128, NK, T], BF16)
    vtm = p.sbuf("vtm", [128, 16, HD], BF16)
    wT = p.sbuf("wT", [128, 16, TT], BF16)
    rs = p.sbuf("rs", [1, TT], F32)
    rb_sb = p.sbuf("rb_sb", [64, TT], F32)
    aos = p.sbuf("aos", [128, NK, T], BF16)
    idb_sb = p.sbuf("idb_sb", [128, 128], BF16)
    onesb = p.sbuf("onesb", [128, 1], BF16)
    ones_row = p.sbuf("ones_row", [1, 128], F32)
    ps8 = p.psum("ps8", [128, 8, TT], F32)

    p.dma_sec_n('sync', 1 + 3 * NK, lambda e: [
        e.dma_start(out=idb_sb[:, :], in_=idb[:, :])] + [
        e.dma_start(out=t[:, k, :], in_=a[k * 128:(k + 1) * 128, :])
        for t, a in ((qs, q4), (ks, k4), (vs, v4)) for k in range(NK)])
    p.sec('vector', lambda v: [v.memset(onesb[:, :], 1.0),
                               v.memset(ones_row[:, :], 1.0)])

    for h in range(NHEAD):
        half, oc = (h % 2) * 64, h // 2
        for r in range(2):
            def t_tr(te, r=r, half=half, oc=oc):
                out = []
                for j in range(8):
                    kc = r * 8 + j
                    out.append(te.transpose(
                        ps8[:, j, 0:32].bitcast(BF16),
                        vs[half:half + 64, oc, kc * 128:(kc + 1) * 128],
                        idb_sb[half:half + 64, half:half + 64]))
                return out
            p.sec('tensor', t_tr)

            def v_tr(v, r=r):
                return [v.tensor_copy(vtm[:, r * 8 + j, :],
                                      ps8[:, j, 0:32].bitcast(BF16))
                        for j in range(8)]
            p.sec('vector', v_tr)

        for qt in range(NT):
            qsl = slice(qt * TT, (qt + 1) * TT)
            for r in range(2):
                def t_s(te, r=r, half=half, oc=oc, qsl=qsl):
                    out = []
                    for j in range(8):
                        kc = r * 8 + j
                        out.append(te.matmul(
                            ps8[:, j, :], ks[half:half + 64, oc, kc * 128:(kc + 1) * 128],
                            qs[half:half + 64, oc, qsl], start=True, stop=True))
                    return out
                p.sec('tensor', t_s)

                def s_s(s, r=r):
                    return [s.activation(wT[:, r * 8 + j, :], ps8[:, j, :],
                                         AF.Relu, scale=0.125) for j in range(8)]
                p.sec('scalar', s_s)
            def t_r(te):
                for kc in range(16):
                    ins = te.matmul(ps8[0:1, 0, :], onesb[:, :], wT[:, kc, :],
                                    start=(kc == 0), stop=(kc == 15))
                return ins
            p.sec('tensor', t_r)

            def v_r(v):
                v.tensor_copy(rs[:, :], ps8[0:1, 0, :])
                v.tensor_scalar_add(rs[:, :], rs[:, :], 1e-6)
                return v.reciprocal(rs[:, :], rs[:, :])
            p.sec('vector', v_r)
            p.sec('tensor', lambda te: te.matmul(ps8[0:64, 1, :], ones_row[0:1, 0:64],
                                                 rs[0:1, :], start=True, stop=True))
            p.sec('vector', lambda v: v.tensor_copy(rb_sb[:, :], ps8[0:64, 1, :]))

            def t_o(te):
                for kc in range(16):
                    ins = te.matmul(ps8[0:64, 2, :], vtm[:, kc, :], wT[:, kc, :],
                                    start=(kc == 0), stop=(kc == 15))
                return ins
            p.sec('tensor', t_o)
            p.sec('vector', lambda v, half=half, oc=oc, qsl=qsl: v.tensor_tensor(
                aos[half:half + 64, oc, qsl], ps8[0:64, 2, :], rb_sb[:, :], ALU.mult))

    p.dma_sec_n('sync', NK, lambda e: [
        e.dma_start(out=ao[k * 128:(k + 1) * 128, :], in_=aos[:, k, :])
        for k in range(NK)])
    p.fin()
    return p.close()


def build_attnout():
    p = Prog("aout").start()
    nc = p.nc
    at = nc.dram_tensor("attnout", [D, T], BF16, kind="ExternalInput").ap()
    xl = nc.dram_tensor("xln4", [D, T], F32, kind="ExternalInput").ap()
    x4 = nc.dram_tensor("xT", [D, T], F32, kind="ExternalInput").ap()
    wo = nc.dram_tensor("w_out", [D, D], F32, kind="ExternalInput").ap()
    bo = nc.dram_tensor("b_out_r", [128, 8], F32, kind="ExternalInput").ap()
    ga = nc.dram_tensor("gamma_r", [128, 8], F32, kind="ExternalInput").ap()
    be = nc.dram_tensor("beta_r", [128, 8], F32, kind="ExternalInput").ap()
    x5o = nc.dram_tensor("x_out", [D, T], F32, kind="ExternalOutput").ap()

    xs = p.sbuf("xs", [128, NK, T], F32)
    sq = p.sbuf("sq", [128, 3, TT], F32)
    zt = p.sbuf("zt", [128, T], F32)
    ats = p.sbuf("ats", [128, NK, TT], BF16)
    wos = p.sbuf("wos", [128, NK, D], BF16)
    bos = p.sbuf("bos", [128, 8], F32)
    gas = p.sbuf("gas", [128, 8], F32)
    bes = p.sbuf("bes", [128, 8], F32)
    x4t = p.sbuf("x4t", [128, T], F32)
    work = p.sbuf("work", [1, 3 * T], F32)
    ones_col = p.sbuf("ones_col", [128, 1], F32)
    ones_row = p.sbuf("ones_row", [1, 128], F32)
    alloc_ln(p)
    ps_sx = p.psum("ps_sx", [1, TT], F32)
    ps_sxx = p.psum("ps_sxx", [1, TT], F32)
    ps_bc = p.psum("ps_bc", [128, 2 * TT], F32)
    ps_a = p.psum("ps_a", [128, TT], F32)

    p.sec('vector', lambda v: [v.memset(ones_col[:, :], 1.0),
                               v.memset(ones_row[:, :], 1.0)])
    p.dma_sec_n('sync', 3 + NK, lambda e: [
        e.dma_start(out=bos[:, :], in_=bo[:, :]),
        e.dma_start(out=gas[:, :], in_=ga[:, :]),
        e.dma_start(out=bes[:, :], in_=be[:, :])] + [
        e.dma_start(out=xs[:, k, :], in_=xl[k * 128:(k + 1) * 128, :])
        for k in range(NK)])
    p.dma_sec_n('gpsimd', 1, lambda e: [
        e.dma_start(out=wos[:, :, :], in_=wo.rearrange("(k p) o -> p k o", p=128))])

    for tt_i in range(NT):
        tsl = slice(tt_i * TT, (tt_i + 1) * TT)
        p.dma_sec_n('sync', 1, lambda e, tsl=tsl: [
            e.dma_start(out=ats[:, :, :],
                        in_=at[:, tsl].rearrange("(k p) c -> p k c", p=128))])
        for oc in range(NK):
            def t_mm(te, oc=oc):
                for k in range(NK):
                    ins = te.matmul(ps_a[:, :], wos[:, k, oc * 128:(oc + 1) * 128],
                                    ats[:, k, :], start=(k == 0), stop=(k == NK - 1))
                return ins
            p.sec('tensor', t_mm)

            def v_y(v, oc=oc, tsl=tsl):
                v.tensor_scalar(sq[:, 0, 0:TT], ps_a[:, :], bos[:, oc:oc + 1],
                                None, ALU.add)
                return v.tensor_tensor(xs[:, oc, tsl], xs[:, oc, tsl],
                                       sq[:, 0, 0:TT], ALU.add)
            p.sec('vector', v_y)

    lnf = p.sbuf("lnf", [128, NK, T], BF16)
    emit_ln(p, xs, lnf, sq, work, ps_sx, ps_sxx, ps_bc, ones_col, ones_row)

    for k in range(NK):
        p.dma_sec_n('sync', 1, lambda e, k=k: [
            e.dma_start(out=x4t[:, :], in_=x4[k * 128:(k + 1) * 128, :])])

        def v_z(v, k=k):
            v.tensor_scalar(zt[:, :], lnf[:, k, :], gas[:, k:k + 1], None, ALU.mult)
            v.tensor_scalar(zt[:, :], zt[:, :], bes[:, k:k + 1], None, ALU.add)
            return v.tensor_tensor(xs[:, k, :], x4t[:, :], zt[:, :], ALU.add)
        p.sec('vector', v_z)
        p.dma_sec_n('sync', 1, lambda e, k=k: [
            e.dma_start(out=x5o[k * 128:(k + 1) * 128, :], in_=xs[:, k, :])])

    emit_ln(p, xs, lnf, sq, work, ps_sx, ps_sxx, ps_bc, ones_col, ones_row)
    gw = nc.dram_tensor("gwT", [D, 8], F32, kind="ExternalInput").ap()
    idf = nc.dram_tensor("identf", [128, 128], F32, kind="ExternalInput").ap()
    gates_out = nc.dram_tensor("gates", [8, T], F32, kind="ExternalOutput").ap()
    lnout_hbm = nc.dram_tensor("lnbf_out", [D, T], BF16, kind="ExternalOutput").ap()
    gw_sb = p.sbuf("gw_sb", [128, NK, 8], BF16)
    idf_sb = p.sbuf("idf_sb", [128, 128], F32)
    gfm = p.sbuf("gfm", [8, T], F32)
    ptm = p.sbuf("ptm", [128, 16, 26], F32)
    negt = p.sbuf("negt", [128, 8], F32)
    p.dma_sec_n('sync', 1, lambda e: [e.dma_start(out=idf_sb[:, :], in_=idf[:, :])])
    p.dma_sec_n('gpsimd', 1, lambda e: [
        e.dma_start(out=gw_sb[:, :, :], in_=gw.rearrange("(k p) e -> p k e", p=128))])
    p.sec('vector', lambda v: v.memset(negt[:, :], -1e30))
    emit_logits(p, lnf, gw_sb, 8, ps_a, gfm, bias_sb=None)
    emit_softmax_gates(p, gfm, gfm, ptm, ps_bc, idf_sb, negt, 'probs')
    p.dma_sec_n('sync', 1 + NK, lambda e: [
        e.dma_start(out=gates_out[:, :], in_=gfm[:, :])] + [
        e.dma_start(out=lnout_hbm[k * 128:(k + 1) * 128, :], in_=lnf[:, k, :])
        for k in range(NK)])
    p.fin()
    return p.close()


def build_final():
    p = Prog("final").start()
    nc = p.nc
    xT = nc.dram_tensor("xT", [D, T], F32, kind="ExternalInput").ap()
    pp = nc.dram_tensor("pparts", [8, D, T], F32, kind="ExternalInput").ap()
    xo = nc.dram_tensor("x_out", [D, T], F32, kind="ExternalOutput").ap()
    xt = p.sbuf("xt", [128, T], F32)
    pt = p.sbuf("pt", [128, 2, T], F32)
    acc = p.sbuf("acc", [128, T], F32)
    for k in range(NK):
        p.dma_sec_n('sync', 1, lambda e, k=k: [
            e.dma_start(out=xt[:, :], in_=xT[k * 128:(k + 1) * 128, :])])
        for j0 in range(0, 8, 2):
            p.dma_sec_n('sync', 2, lambda e, k=k, j0=j0: [
                e.dma_start(out=pt[:, j % 2, :], in_=pp[j, k * 128:(k + 1) * 128, :])
                for j in range(j0, j0 + 2)])

            def v_acc(v, j0=j0):
                out = []
                for j in range(j0, j0 + 2):
                    if j == 0:
                        out.append(v.tensor_copy(acc[:, :], pt[:, 0, :]))
                    else:
                        out.append(v.tensor_tensor(acc[:, :], acc[:, :],
                                                   pt[:, j % 2, :], ALU.add))
                return out
            p.sec('vector', v_acc)
        p.sec('vector', lambda v: v.scalar_tensor_tensor(
            xt[:, :], acc[:, :], 0.5, xt[:, :], ALU.mult, ALU.add))
        p.dma_sec_n('sync', 1, lambda e, k=k: [
            e.dma_start(out=xo[k * 128:(k + 1) * 128, :], in_=xt[:, :])])
    p.fin()
    return p.close()


_PROGS = {}


def _get(name, fn):
    if name not in _PROGS:
        _PROGS[name] = fn()
    return _PROGS[name]


def _run(nc, maps):
    res = run_bass_kernel_spmd(nc, maps, core_ids=list(range(8)))
    return res.results


def _consts():
    c = {}
    c['identf'] = np.eye(128, dtype=np.float32)
    c['identb'] = np.eye(128, dtype=np.float32).astype(BF)
    bd2 = np.zeros((128, 2), np.float32)
    bd2[0:64, 0] = 1.0
    bd2[64:128, 1] = 1.0
    c['blockdiag2'] = bd2
    bc = np.zeros((2, 128), np.float32)
    bc[0, 0:64] = 1.0
    bc[1, 64:128] = 1.0
    c['bcast64'] = bc
    sel8 = np.zeros((8, 1024), np.float32)
    for e in range(8):
        sel8[e, e * 128:(e + 1) * 128] = 1.0
    c['sel8'] = sel8
    go = np.zeros((128, 8), np.float32)
    sg = np.zeros((8, 128), np.float32)
    pk = np.zeros((128, 8), np.float32)
    for e in range(8):
        go[16 * e:16 * (e + 1), e] = 1.0
        sg[e, 16 * e:16 * (e + 1)] = 1.0
        pk[16 * e, e] = 1.0
    c['groupones'] = go
    c['selg'] = sg
    c['pick'] = pk
    return c


def kernel(**inp):
    f = lambda a: np.ascontiguousarray(np.asarray(a, dtype=np.float32))
    x = f(inp['L0_attn_x'])
    C = _consts()
    N = 8

    xT = [np.ascontiguousarray(x[i].T) for i in range(N)]

    gwT = f(inp['L0_ffn_gate_w'].T)
    gb = f(inp['L0_ffn_gate_b'])
    w1 = f(inp['L0_ffn_w1'])
    w2 = f(inp['L0_ffn_w2'])
    b1r = np.ascontiguousarray(f(inp['L0_ffn_b1']).reshape(8, 32, 128).transpose(0, 2, 1))
    b2r = np.ascontiguousarray(f(inp['L0_ffn_b2']).reshape(8, 8, 128).transpose(0, 2, 1))
    WqT = f(inp['L0_attn_Wq'].T)
    WkT = f(inp['L0_attn_Wk'].T)
    WvT = f(inp['L0_attn_Wv'].T)
    WoT = f(inp['L0_attn_Wo'].T)
    WbetaT = f(inp['L0_attn_Wbeta'].T)
    rmsw_rep = np.ascontiguousarray(np.tile(f(inp['L0_attn_rms_weight'])[None, :], (128, 1)))
    Wg = f(inp['L1_ffn_Wg'])
    W1 = f(inp['L1_ffn_W1'])
    W2 = f(inp['L1_ffn_W2'])
    perm = np.array([h * 192 + o * 64 + d for o in range(3) for h in range(16)
                     for d in range(64)])
    wqkvP = np.ascontiguousarray(f(inp['L1_attn_w_qkv'])[:, perm])
    w_out = f(inp['L1_attn_w_out'])
    b_out_r = np.ascontiguousarray(f(inp['L1_attn_b_out']).reshape(8, 128).T)
    gamma_r = np.ascontiguousarray(f(inp['L1_attn_gamma']).reshape(8, 128).T)
    beta_r = np.ascontiguousarray(f(inp['L1_attn_beta']).reshape(8, 128).T)

    P_pre = _get('pre', lambda: build_stage(0, {'gates0'}))
    P_expL0 = _get('expL0', lambda: build_experts('L0', 2))
    P_comb_proj = _get('comb_proj', lambda: build_stage(8, {'out_x', 'projL0'}))
    P_scan = _get('scan', build_scan)
    P_rmswo = _get('rmswo', build_rmswo)
    P_comb_probs = _get('comb_probs', lambda: build_stage(8, {'out_x', 'probs1'}))
    P_route = _get('route', build_route)
    P_expL1 = _get('expL1', lambda: build_experts('L1', 4))
    P_comb_qkv = _get('comb_qkv', lambda: build_stage(8, {'out_x', 'qkv1'}))
    P_attn = _get('attn', build_attn1)
    P_aout = _get('aout', build_attnout)
    P_final = _get('final', build_final)

    def ffn0(lnb, gts):
        parts = [None] * N
        for g in range(4):
            r = _run(P_expL0, [{
                'lnbf_in': lnb[i], 'gates': gts[i],
                'sel2': np.ascontiguousarray(C['sel8'][:, g * 256:(g + 1) * 256]),
                'w1': w1[2 * g:2 * g + 2], 'w2': w2[2 * g:2 * g + 2],
                'b1': b1r[2 * g:2 * g + 2], 'b2': b2r[2 * g:2 * g + 2],
            } for i in range(N)])
            for i in range(N):
                pi = r[i]['partials']
                parts[i] = pi if g == 0 else np.concatenate([parts[i], pi], 0)
        return parts

    def ffn1(lnb, gts):
        parts = [None] * N
        for g in range(2):
            r = _run(P_expL1, [{
                'lnbf_in': lnb[i], 'gates': gts[i],
                'sel2': np.ascontiguousarray(C['sel8'][:, g * 512:(g + 1) * 512]),
                'w1': W1[4 * g:4 * g + 4], 'w2': W2[4 * g:4 * g + 4],
            } for i in range(N)])
            for i in range(N):
                pi = r[i]['partials']
                parts[i] = pi if g == 0 else np.concatenate([parts[i], pi], 0)
        return parts

    def route(probs):
        pall = np.concatenate(probs, axis=1)
        p_resh = np.ascontiguousarray(pall.reshape(8, 16, 1024).reshape(128, 1024))
        r = _run(P_route, [{
            'p_resh': p_resh, 'probs_own': probs[i], 'groupones': C['groupones'],
            'selg': C['selg'], 'pick': C['pick'], 'identf': C['identf'],
        } for i in range(N)])
        return [r[i]['gates'] for i in range(N)]

    r = _run(P_pre, [{'xT': xT[i], 'gwT': gwT, 'gb': gb, 'identf': C['identf']}
                     for i in range(N)])
    ln0 = [r[i]['lnbf_out'] for i in range(N)]
    gates0 = [r[i]['gates'] for i in range(N)]

    parts = ffn0(ln0, gates0)

    r = _run(P_comb_proj, [{
        'xT': xT[i], 'pparts': parts[i], 'WqT': WqT, 'WkT': WkT, 'WvT': WvT,
        'WbetaT': WbetaT, 'blockdiag2': C['blockdiag2'], 'bcast64': C['bcast64'],
        'identf': C['identf'],
    } for i in range(N)])
    x1 = [r[i]['x_out'] for i in range(N)]
    qlm = [r[i]['q_lm'] for i in range(N)]
    klm = [r[i]['k_lm'] for i in range(N)]
    vlm = [r[i]['v_lm'] for i in range(N)]
    beta = [r[i]['beta'] for i in range(N)]

    M = [np.zeros((16, 4096), np.float32)] * N
    oparts = [[] for _ in range(N)]
    for part in range(4):
        sl = slice(part * 512, (part + 1) * 512)
        r = _run(P_scan, [{
            'qp': np.ascontiguousarray(qlm[i][:, sl, :]),
            'kp': np.ascontiguousarray(klm[i][:, sl, :]),
            'vp': np.ascontiguousarray(vlm[i][:, sl, :]),
            'betap': np.ascontiguousarray(beta[i][:, sl]),
            'M_in': M[i],
        } for i in range(N)])
        M = [r[i]['M_out'] for i in range(N)]
        for i in range(N):
            oparts[i].append(r[i]['o_part'])

    r = _run(P_rmswo, [{
        'o_full': np.ascontiguousarray(np.concatenate(oparts[i], 0)),
        'xT': x1[i], 'WoT': WoT, 'rmsw_rep': rmsw_rep, 'identb': C['identb'],
        'gwT': gwT, 'gb': gb, 'identf': C['identf'],
    } for i in range(N)])
    x2 = [r[i]['x_out'] for i in range(N)]
    ln2 = [r[i]['lnbf_out'] for i in range(N)]
    gates2 = [r[i]['gates'] for i in range(N)]

    parts = ffn0(ln2, gates2)

    r = _run(P_comb_probs, [{'xT': x2[i], 'pparts': parts[i], 'gwT': Wg,
                             'identf': C['identf']} for i in range(N)])
    x3 = [r[i]['x_out'] for i in range(N)]
    ln3 = [r[i]['lnbf_out'] for i in range(N)]
    probs3 = [r[i]['gates'] for i in range(N)]

    gates3 = route(probs3)
    parts = ffn1(ln3, gates3)

    r = _run(P_comb_qkv, [{'xT': x3[i], 'pparts': parts[i], 'wqkvP': wqkvP}
                          for i in range(N)])
    x4 = [r[i]['x_out'] for i in range(N)]
    xln4 = [r[i]['xln_out'] for i in range(N)]
    q4 = [r[i]['q4'] for i in range(N)]
    k4 = [r[i]['k4'] for i in range(N)]
    v4 = [r[i]['v4'] for i in range(N)]

    r = _run(P_attn, [{'q4': q4[i], 'k4': k4[i], 'v4': v4[i],
                       'identb': C['identb']} for i in range(N)])
    ao = [r[i]['attnout'] for i in range(N)]
    r = _run(P_aout, [{'attnout': ao[i], 'xln4': xln4[i], 'xT': x4[i],
                       'w_out': w_out, 'b_out_r': b_out_r, 'gamma_r': gamma_r,
                       'beta_r': beta_r, 'gwT': Wg, 'identf': C['identf']}
                      for i in range(N)])
    x5 = [r[i]['x_out'] for i in range(N)]
    ln5 = [r[i]['lnbf_out'] for i in range(N)]
    probs5 = [r[i]['gates'] for i in range(N)]

    gates5 = route(probs5)
    parts = ffn1(ln5, gates5)

    r = _run(P_final, [{'xT': x5[i], 'pparts': parts[i]} for i in range(N)])
    out = np.stack([np.asarray(r[i]['x_out']).T for i in range(N)], axis=0)
    return np.ascontiguousarray(out.astype(np.float32))
